# version 08
# baseline (speedup 1.0000x reference)
import numpy as np
import ml_dtypes
from contextlib import ExitStack

import concourse.bass as bass
import concourse.mybir as mybir
from concourse.bass_utils import run_bass_kernel_spmd

F32 = mybir.dt.float32
BF16 = mybir.dt.bfloat16
I32 = mybir.dt.int32
AF = mybir.ActivationFunctionType
ALU = mybir.AluOpType
AX = mybir.AxisListType.X

NCORE = 8
D = 2048
TC = 1024
NT = 8
HCH = 16
FF = 1408
NFC = 11
NEXP = 10
NH = 16
HL = 4
SEQ = 4096
NQB = 8
ISQ = 0.08838834764831845

GU_SLOTS = [0, 2048, 4096, 6144]
D_SLOTS = [8192, 13824]
QS_SLOTS = [8192, 12288]
WP_ELS = 19456


class Sched:
    ENGINES = ("t", "v", "s", "g", "y")

    def __init__(self):
        self.q = {k: [] for k in self.ENGINES}
        self.cnt = {k: 0 for k in ("SW", "SY", "SS", "SP", "SV", "SA", "SG",
                                   "SC", "SC2")}
        self.sems = {}
        self.greg = None

    def op(self, eng, fn, waits=(), inc=None, inc_n=1):
        ws = tuple((s, v) for (s, v) in waits if s is not None and v and v > 0)
        if inc is not None:
            self.cnt[inc] += inc_n
        self.q[eng].append((ws, fn, inc, inc_n))
        return self.cnt[inc] if inc is not None else None

    def v(self, sem):
        return self.cnt[sem]

    def run(self, eng, e):
        waited = {}
        for ws, fn, inc, inc_n in self.q[eng]:
            for s, val in ws:
                if val > waited.get(s, 0):
                    e.wait_ge(self.sems[s], val)
                    waited[s] = val
            try:
                inst = fn(e) if fn is not None else None
            except Exception:
                import traceback
                traceback.print_exc()
                raise
            if inc is not None:
                inst.then_inc(self.sems[inc], inc_n)


def build_nc(stages=("ffn1", "attn", "ffn2"), experts=tuple(range(NEXP))):
    nc = bass.Bass("TRN2", target_bir_lowering=False)

    x_t = nc.dram_tensor("x", [TC, D], F32, kind="ExternalInput")
    wqkvl_t = nc.dram_tensor("wqkvl", [D, 1536], F32, kind="ExternalInput")
    bqkvl_t = nc.dram_tensor("bqkvl", [1, 1536], F32, kind="ExternalInput")
    wo_t = nc.dram_tensor("wo", [D, D], F32, kind="ExternalInput")
    bo_t = nc.dram_tensor("bo", [1, D], F32, kind="ExternalInput")
    gateT_t = nc.dram_tensor("gateT", [D, 8], F32, kind="ExternalInput")
    guT_t = nc.dram_tensor("guT", [8, D, 2 * FF], F32, kind="ExternalInput")
    dT_t = nc.dram_tensor("dT", [8, FF, D], F32, kind="ExternalInput")
    sguT_t = nc.dram_tensor("sguT", [D, 4 * FF], F32, kind="ExternalInput")
    sdT_t = nc.dram_tensor("sdT", [2 * FF, D], F32, kind="ExternalInput")
    iden_t = nc.dram_tensor("iden", [128, 128], BF16, kind="ExternalInput")
    dmask_t = nc.dram_tensor("dmask", [128, 2048], BF16, kind="ExternalInput")
    meta_t = nc.dram_tensor("meta", [1, 16], I32, kind="ExternalInput")
    y_t = nc.dram_tensor("y", [TC, D], F32, kind="ExternalOutput")

    xn2c = nc.dram_tensor("xn2c", [2 * TC, TC], BF16)
    xn2x = nc.dram_tensor("xn2x", [16 * TC, TC], BF16, addr_space="Shared")
    q_l = nc.dram_tensor("q_l", [512, SEQ], BF16)
    k_l = nc.dram_tensor("k_l", [512, SEQ], BF16)
    v_l = nc.dram_tensor("v_l", [SEQ, 512], BF16)
    ctx_c = nc.dram_tensor("ctx_c", [2 * TC, TC], BF16)
    ctx_x = nc.dram_tensor("ctx_x", [16 * TC, TC], BF16, addr_space="Shared")
    GROUPS = [[0, 1, 2, 3], [4, 5, 6, 7]]

    S = Sched()

    with ExitStack() as ectx:
        e_ = ectx.enter_context
        r_ = e_(nc.sbuf_tensor("r", [128, NT * D], F32))
        xn_ = e_(nc.sbuf_tensor("xn", [128, NT * D], BF16))
        fm_ = e_(nc.sbuf_tensor("fm", [128, HCH * TC], BF16))
        wp_ = e_(nc.sbuf_tensor("wp", [128, WP_ELS], BF16))
        act_ = e_(nc.sbuf_tensor("act", [128, NFC * TC], BF16))
        stg_ = [e_(nc.sbuf_tensor(f"stg{i}", [128, 512], BF16)) for i in range(2)]
        scr_ = e_(nc.sbuf_tensor("scr", [128, 512], F32))
        ctxf_ = e_(nc.sbuf_tensor("ctxf", [128, 512], F32))
        idn_ = e_(nc.sbuf_tensor("idn", [128, 128], BF16))
        dm_ = e_(nc.sbuf_tensor("dm", [128, 2048], BF16))
        gat_ = e_(nc.sbuf_tensor("gat", [128, HCH * 8], BF16))
        onesr_ = e_(nc.sbuf_tensor("onesr", [128, 512], BF16))
        onesc_ = e_(nc.sbuf_tensor("onesc", [128, 1], BF16))
        onesf_ = e_(nc.sbuf_tensor("onesf", [1, 128], F32))
        ceps_ = e_(nc.sbuf_tensor("ceps", [128, 1], F32))
        bias4_ = e_(nc.sbuf_tensor("bias4", [1, 3584], BF16))
        meta_ = e_(nc.sbuf_tensor("metas", [1, 16], I32))
        ps_ = [e_(nc.psum_tensor(f"ps{i}", [128, 512], F32)) for i in range(6)]
        psB_ = e_(nc.psum_tensor("psB", [128, 512], F32))
        psT_ = [e_(nc.psum_tensor("psT0", [128, 1024], BF16))]
        for name in S.cnt:
            S.sems[name] = e_(nc.semaphore(name))

        hcb_ = scr_[:, 0:64]
        rt_ = scr_[:, 64:112]
        lnst_ = scr_[:, 112:144]
        rec_ = scr_

        G = {}

        def guard(name):
            g = G.get(name)
            return [g] if g else []

        def set_guard(name, sem):
            G[name] = (sem, S.v(sem))

        def merge_guard(name, *names):
            best = None
            for n in names:
                g = G.get(n)
                if g and (best is None or g[1] > best[1]):
                    best = g
            if best:
                G[name] = best

        def g_dma(fn, waits=()):
            return S.op("g", fn, waits=waits, inc="SW", inc_n=16)

        def y_load(fn, waits=()):
            return S.op("y", fn, waits=waits, inc="SY", inc_n=16)

        def y_store(fn, waits=()):
            return S.op("y", fn, waits=waits, inc="SS", inc_n=16)

        for xh in range(2):
            y_load(lambda e, xh=xh: e.dma_start(
                out=r_[:, xh * 4 * D:(xh + 1) * 4 * D].rearrange(
                    "p (c f) -> p c f", f=D),
                in_=x_t.ap().rearrange("(c p) f -> p c f", p=128)[
                    :, xh * 4:(xh + 1) * 4, :]))
        y_load(lambda e: e.dma_start(out=idn_[:, :], in_=iden_t.ap()))
        y_load(lambda e: e.dma_start(out=dm_[:, :], in_=dmask_t.ap()))
        y_meta = y_load(lambda e: e.dma_start(out=meta_[:, :], in_=meta_t.ap()))
        S.op("g", lambda e: e.memset(onesr_[:, :], 1.0), inc="SG")
        S.op("g", lambda e: e.memset(onesc_[:, :], 1.0), inc="SG")
        S.op("g", lambda e: e.memset(onesf_[:, :], 1.0), inc="SG")
        S.op("g", lambda e: e.memset(ceps_[:, :], 1e-5), inc="SG")
        sg_pre = S.v("SG")
        g_dma(lambda e: e.dma_start(
            out=gat_[:, :].rearrange("p (c e) -> p c e", e=8),
            in_=gateT_t.ap().rearrange("(c p) e -> p c e", p=128)))
        g_dma(lambda e: e.dma_start(out=bias4_[0:1, 0:1536], in_=bqkvl_t.ap()))
        w_pre = g_dma(lambda e: e.dma_start(out=bias4_[0:1, 1536:3584], in_=bo_t.ap()))
        pre_t = [("SY", 80), ("SG", sg_pre), ("SW", w_pre)]
        pre_h = [[("SY", 16), ("SG", sg_pre), ("SW", w_pre)],
                 [("SY", 32), ("SG", sg_pre), ("SW", w_pre)]]

        cur = {"gupair": 0, "dstrip": 0, "qs": 0, "ps03": 0, "stg": 0}

        def next_stg():
            i = cur["stg"] % 2
            cur["stg"] += 1
            return stg_[i], f"stg{i}"

        def emit_ln():
            def half_stats(half):
                base = half * 4
                for tc in range(base, base + 4):
                    rsl = r_[:, tc * D:(tc + 1) * D]
                    sq = act_[:, (tc % 2) * D:(tc % 2) * D + D]
                    w = guard(f"sq{tc % 2}") + pre_h[half] + guard("r")
                    if tc == 0:
                        w = w + guard("act_r")
                    S.op("s", lambda e, o=sq, i=rsl: e.activation(o, i, AF.Square),
                         waits=w, inc="SA")
                    sa_sq = S.v("SA")
                    S.op("v", lambda e, o=lnst_[:, 4 * tc:4 * tc + 1], i=rsl:
                         e.reduce_sum(o, i, axis=AX),
                         waits=pre_h[half] + guard("r"), inc="SV")
                    S.op("v", lambda e, o=lnst_[:, 4 * tc + 1:4 * tc + 2], i=sq:
                         e.reduce_sum(o, i, axis=AX), waits=[("SA", sa_sq)],
                         inc="SV")
                    set_guard(f"sq{tc % 2}", "SV")

                def st_(j):
                    return lnst_.rearrange("p (a b) -> p a b", b=4)[
                        :, base:base + 4, j:j + 1]

                def rt3(a, b):
                    return rt_[:, a:b].rearrange("p (a b) -> p a b", b=1)
                ra, rb = 4 * half, 4 * half + 4
                S.op("v", lambda e: e.tensor_scalar(st_(2), st_(0), -1.0 / D, None,
                                                    ALU.mult),
                     waits=[("SV", S.v("SV"))], inc="SV")
                sv_negm = S.v("SV")
                S.op("s", lambda e: e.activation(rt3(ra, rb), st_(2), AF.Square),
                     waits=[("SV", sv_negm)], inc="SA")
                sa_msq = S.v("SA")
                S.op("v", lambda e: e.tensor_scalar(rt3(8 + ra, 8 + rb), st_(1),
                                                    1.0 / D, None, ALU.mult),
                     inc="SV")
                S.op("v", lambda e: e.tensor_sub(rt3(16 + ra, 16 + rb),
                                                 rt3(8 + ra, 8 + rb),
                                                 rt3(ra, rb)),
                     waits=[("SA", sa_msq), ("SV", S.v("SV"))], inc="SV")
                sv_var = S.v("SV")
                S.op("s", lambda e: e.activation(rt3(24 + ra, 24 + rb),
                                                 rt3(16 + ra, 16 + rb), AF.Ln,
                                                 bias=ceps_[:, :]),
                     waits=[("SV", sv_var)], inc="SA")
                S.op("s", lambda e: e.activation(st_(3), rt3(24 + ra, 24 + rb),
                                                 AF.Exp, scale=-0.5), inc="SA")
                sa_rstd = S.v("SA")
                for tc in range(base, base + 4):
                    w = [("SA", sa_rstd)]
                    if tc == 0:
                        w = w + guard("xn_r")
                    S.op("v", lambda e, tc=tc: e.tensor_scalar(
                        xn_[:, tc * D:(tc + 1) * D], r_[:, tc * D:(tc + 1) * D],
                        lnst_[:, 4 * tc + 2:4 * tc + 3],
                        lnst_[:, 4 * tc + 3:4 * tc + 4],
                        ALU.add, ALU.mult), waits=w, inc="SV")
                return S.v("SV")

            def half_trans(half, sv_xn):
                base = half * 4
                for hc in range(HCH):
                    pst = psT_[0]
                    for k in range(4):
                        tc = base + k
                        w = []
                        if k == 0:
                            w = [("SV", sv_xn), ("SY", 48)] + guard("psT0")
                            if hc == 0 and half == 0:
                                w = w + guard("fm_r")
                        S.op("t", lambda e, pst=pst, tc=tc, hc=hc, k=k:
                             e.transpose(
                                 pst[:, k * 128:(k + 1) * 128],
                                 xn_[:, tc * D + hc * 128: tc * D + hc * 128 + 128],
                                 idn_[:, :]),
                             waits=w, inc="SP" if k == 3 else None)
                    sp_tr = S.v("SP")
                    w = [("SP", sp_tr)]
                    if hc == 0 and half == 0:
                        w = w + guard("fm_r")
                    S.op("v", lambda e, hc=hc, half=half, pst=pst: e.tensor_copy(
                        fm_[:, hc * TC + half * 512: hc * TC + half * 512 + 512],
                        pst[:, 0:512]), waits=w, inc="SV")
                    set_guard("psT0", "SV")

            sv0 = half_stats(0)
            sv1 = half_stats(1)
            half_trans(0, sv0)
            half_trans(1, sv1)
            set_guard("xn_r", "SP")
            return S.v("SV")

        def emit_ffn(experts=tuple(range(NEXP))):
            sv_fm = emit_ln()
            last_sa_e = 0
            for tc in range(NT):
                psl = ps_[4 + tc % 2]
                for hc in range(HCH):
                    w = []
                    if hc == 0:
                        w = [("SV", sv_fm)] + guard(f"ps{4 + tc % 2}") + pre_t
                    S.op("t", lambda e, psl=psl, hc=hc, tc=tc: e.matmul(
                        psl[:, 0:8],
                        fm_[:, hc * TC + tc * 128: hc * TC + tc * 128 + 128],
                        gat_[:, hc * 8:(hc + 1) * 8],
                        start=(hc == 0), stop=(hc == HCH - 1)),
                        waits=w, inc="SP" if hc == HCH - 1 else None)
                sp_l = S.v("SP")

                def vs(fn, extra=()):
                    S.op("v", fn, waits=[("SV", S.v("SV"))] + list(extra), inc="SV")
                vs(lambda e, psl=psl: e.tensor_copy(rt_[:, 0:8], psl[:, 0:8]),
                   [("SP", sp_l), ("SA", last_sa_e)])
                set_guard(f"ps{4 + tc % 2}", "SV")
                vs(lambda e: e.reduce_max(rt_[:, 8:9], rt_[:, 0:8], axis=AX))
                vs(lambda e: e.tensor_scalar(rt_[:, 13:14], rt_[:, 8:9],
                                             -1.0, None, ALU.mult))
                sv_m1 = S.v("SV")
                S.op("s", lambda e: e.activation(rt_[:, 16:24], rt_[:, 0:8], AF.Exp,
                                                 bias=rt_[:, 13:14]),
                     waits=[("SV", sv_m1)], inc="SA")
                last_sa_e = S.v("SA")
                vs(lambda e: e.tensor_scalar(rt_[:, 24:32], rt_[:, 0:8],
                                             rt_[:, 8:9], None, ALU.is_ge))
                vs(lambda e: e.scalar_tensor_tensor(
                    rt_[:, 32:40], rt_[:, 24:32], -1e30, rt_[:, 0:8],
                    ALU.mult, ALU.add))
                vs(lambda e: e.reduce_max(rt_[:, 9:10], rt_[:, 32:40], axis=AX))
                vs(lambda e: e.tensor_scalar(rt_[:, 24:32], rt_[:, 0:8],
                                             rt_[:, 9:10], None, ALU.is_ge),
                   [("SA", last_sa_e)])
                vs(lambda e: e.reduce_sum(rt_[:, 10:11], rt_[:, 16:24], axis=AX))
                vs(lambda e: e.reciprocal(rt_[:, 11:12], rt_[:, 10:11]))
                vs(lambda e: e.tensor_scalar(rt_[:, 12:13], rt_[:, 11:12],
                                             0.5, None, ALU.mult))
                vs(lambda e, tc=tc: e.scalar_tensor_tensor(
                    hcb_[:, tc * 8:(tc + 1) * 8], rt_[:, 16:24], rt_[:, 12:13],
                    rt_[:, 24:32], ALU.mult, ALU.mult))
            loads = []
            for ex in experts:
                loads += [("gu", ex, fc) for fc in range(NFC)]
                loads += [("d", ex, hs) for hs in range(4)]
            linfo = {}

            def emit_load(item):
                kind, ex, j = item
                if kind == "gu":
                    k = cur["gupair"]
                    cur["gupair"] += 1
                    s1 = GU_SLOTS[(2 * k) % 4]
                    s2 = GU_SLOTS[(2 * k) % 4 + 1]
                    if ex < 8:
                        src1 = guT_t[ex, :, j * 128:(j + 1) * 128]
                        src2 = guT_t[ex, :, FF + j * 128: FF + j * 128 + 128]
                    else:
                        kk = ex - 8
                        src1 = sguT_t[:, kk * FF + j * 128: kk * FF + j * 128 + 128]
                        src2 = sguT_t[:, 2 * FF + kk * FF + j * 128:
                                      2 * FF + kk * FF + j * 128 + 128]
                    w1 = g_dma(lambda e, s=s1, src=src1: e.dma_start(
                        out=wp_[:, s:s + 2048].rearrange("p (c f) -> p c f", f=128),
                        in_=src.rearrange("(c p) f -> p c f", p=128)),
                        waits=guard(f"gu{s1}"))
                    w2 = g_dma(lambda e, s=s2, src=src2: e.dma_start(
                        out=wp_[:, s:s + 2048].rearrange("p (c f) -> p c f", f=128),
                        in_=src.rearrange("(c p) f -> p c f", p=128)),
                        waits=guard(f"gu{s2}"))
                    linfo[item] = (w1, w2, s1, s2)
                else:
                    ds = D_SLOTS[cur["dstrip"] % 2]
                    cur["dstrip"] += 1
                    if ex < 8:
                        src = dT_t[ex, :, j * 512:(j + 1) * 512]
                    else:
                        kk = ex - 8
                        src = sdT_t[kk * FF:(kk + 1) * FF, j * 512:(j + 1) * 512]
                    wd = g_dma(lambda e, ds=ds, src=src: e.dma_start(
                        out=wp_[:, ds:ds + 5632].rearrange("p (c h) -> p c h", h=512),
                        in_=src.rearrange("(c p) h -> p c h", p=128)),
                        waits=guard(f"ds{ds}"))
                    linfo[item] = (wd, ds)

            pumped = [0]

            def pump(n):
                while pumped[0] < min(n, len(loads)):
                    emit_load(loads[pumped[0]])
                    pumped[0] += 1

            pump(2)
            ui = 0
            for ex in experts:
                for fc in range(NFC):
                    w1, w2, s1, s2 = linfo[("gu", ex, fc)]
                    for th in range(2):
                        par = (fc * 2 + th) % 2
                        pa, pb = ps_[2 * par], ps_[2 * par + 1]
                        for hc in range(HCH):
                            w = []
                            if hc == 0:
                                w = [("SW", w1), ("SV", sv_fm)] + guard(f"ps{2 * par}")
                            S.op("t", lambda e, pa=pa, s1=s1, hc=hc, th=th: e.matmul(
                                pa[:, :], wp_[:, s1 + hc * 128: s1 + hc * 128 + 128],
                                fm_[:, hc * TC + th * 512: hc * TC + th * 512 + 512],
                                start=(hc == 0), stop=(hc == HCH - 1)), waits=w)
                        for hc in range(HCH):
                            w = []
                            if hc == 0:
                                w = [("SW", w2)] + guard(f"ps{2 * par + 1}")
                            S.op("t", lambda e, pb=pb, s2=s2, hc=hc, th=th: e.matmul(
                                pb[:, :], wp_[:, s2 + hc * 128: s2 + hc * 128 + 128],
                                fm_[:, hc * TC + th * 512: hc * TC + th * 512 + 512],
                                start=(hc == 0), stop=(hc == HCH - 1)),
                                waits=w, inc="SP" if hc == HCH - 1 else None)
                        sp_gu = S.v("SP")
                        if th == 1:
                            G[f"gu{s1}"] = ("SP", sp_gu)
                            G[f"gu{s2}"] = ("SP", sp_gu)
                        sg, sgk = next_stg()
                        S.op("s", lambda e, sg=sg, pa=pa: e.activation(
                            sg[:, :], pa[:, :], AF.Silu),
                            waits=[("SP", sp_gu)] + guard(sgk), inc="SA")
                        sa_si = S.v("SA")
                        set_guard(f"ps{2 * par}", "SA")
                        aw = [("SA", sa_si)]
                        if fc == 0 and th == 0:
                            aw = aw + guard("act_r")
                        S.op("v", lambda e, fc=fc, th=th, sg=sg, pb=pb: e.tensor_mul(
                            act_[:, fc * TC + th * 512: fc * TC + th * 512 + 512],
                            sg[:, :], pb[:, :]), waits=aw, inc="SV")
                        set_guard(f"ps{2 * par + 1}", "SV")
                        G[sgk] = ("SV", S.v("SV"))
                    ui += 1
                    pump(ui + 2)
                sv_act = S.v("SV")
                for hs in range(4):
                    wd, ds = linfo[("d", ex, hs)]
                    for tc in range(NT):
                        par = 4 + (hs * NT + tc) % 2
                        pd = ps_[par]
                        for fc in range(NFC):
                            w = []
                            if fc == 0:
                                w = [("SW", wd), ("SV", sv_act)] + guard(f"ps{par}")
                            S.op("t", lambda e, pd=pd, fc=fc, tc=tc, ds=ds: e.matmul(
                                pd[:, :],
                                act_[:, fc * TC + tc * 128: fc * TC + tc * 128 + 128],
                                wp_[:, ds + fc * 512: ds + fc * 512 + 512],
                                start=(fc == 0), stop=(fc == NFC - 1)),
                                waits=w, inc="SP" if fc == NFC - 1 else None)
                        sp_d = S.v("SP")
                        if tc == NT - 1:
                            G[f"ds{ds}"] = ("SP", sp_d)
                        scal = (hcb_[:, tc * 8 + ex: tc * 8 + ex + 1]
                                if ex < 8 else 0.5)
                        S.op("v", lambda e, tc=tc, hs=hs, pd=pd, scal=scal:
                             e.scalar_tensor_tensor(
                                 r_[:, tc * D + hs * 512: tc * D + hs * 512 + 512],
                                 pd[:, :], scal,
                                 r_[:, tc * D + hs * 512: tc * D + hs * 512 + 512],
                                 ALU.mult, ALU.add),
                             waits=[("SP", sp_d)], inc="SV")
                        set_guard(f"ps{par}", "SV")
                    ui += 1
                    pump(ui + 2)
            set_guard("r", "SV")
            set_guard("fm_r", "SP")
            set_guard("act_r", "SP")

        def emit_attn():
            sv_fm = emit_ln()
            merge_guard("wpA", "gu0", "gu2048", "gu4096", "gu6144")
            merge_guard("wpB", "ds8192", "ds13824", "qs8192", "qs12288")
            sxc = y_store(lambda e: e.dma_start(
                out=xn2c.ap().rearrange("(c p) t -> p c t", p=128),
                in_=fm_[:, :].rearrange("p (c t) -> p c t", t=TC)),
                waits=[("SV", sv_fm)])
            S.op("g", lambda e: e.collective_compute(
                "AllGather", ALU.bypass, replica_groups=[list(range(NCORE))],
                ins=[xn2c.ap()], outs=[xn2x.ap()]),
                waits=[("SS", sxc)], inc="SC")
            sc_xn = S.v("SC")
            for xst in range(8):
                slot = (xst % 2) * 8192

                def xload(e, slot=slot, xst=xst):
                    e.reg_load(S.greg, meta_[0:1, 13:14])
                    e.reg_add(S.greg, S.greg,
                              (xst // 2) * 2 * TC * TC + (xst % 2) * 512)
                    return e.dma_start(
                        out=wp_[:, slot:slot + 8192].rearrange(
                            "p (c t) -> p c t", t=512),
                        in_=bass.AP(xn2x, S.greg,
                                    [[TC, 128], [128 * TC, 16], [1, 512]]))
                xw = g_dma(xload, waits=[("SC", sc_xn), ("SY", 80)] +
                           guard("wpA" if slot == 0 else "wpB"))
                for wst in range(6):
                    aslot = (wst % 2) * 4096
                    ww = g_dma(lambda e, aslot=aslot, wst=wst: e.dma_start(
                        out=act_[:, aslot:aslot + 4096].rearrange(
                            "p (c f) -> p c f", f=256),
                        in_=wqkvl_t[:, wst * 256:(wst + 1) * 256].rearrange(
                            "(c p) f -> p c f", p=128)),
                        waits=guard(f"aw{aslot}") +
                        (guard("act_r") if xst == 0 and wst < 2 else []))
                    if wst < 4:
                        for oc in range(2):
                            ocol = wst * 256 + oc * 128
                            par = cur["ps03"] % 4
                            cur["ps03"] += 1
                            pq = ps_[par]
                            for hc in range(HCH):
                                w = []
                                if hc == 0:
                                    w = [("SW", ww), ("SW", xw)] + guard(f"ps{par}")
                                S.op("t", lambda e, pq=pq, aslot=aslot, hc=hc,
                                     oc=oc, slot=slot: e.matmul(
                                         pq[:, :],
                                         act_[:, aslot + hc * 256 + oc * 128:
                                              aslot + hc * 256 + oc * 128 + 128],
                                         wp_[:, slot + hc * 512:
                                             slot + hc * 512 + 512],
                                         start=(hc == 0), stop=False), waits=w)
                            S.op("t", lambda e, pq=pq, ocol=ocol: e.matmul(
                                pq[:, :], bias4_[0:1, ocol:ocol + 128],
                                onesr_[0:1, 0:512], start=False, stop=True),
                                inc="SP")
                            sp_q = S.v("SP")
                            if oc == 1:
                                G[f"aw{aslot}"] = ("SP", sp_q)
                            sg, sgk = next_stg()
                            S.op("v", lambda e, sg=sg, pq=pq: e.tensor_copy(
                                sg[:, :], pq[:, :]),
                                waits=[("SP", sp_q)] + guard(sgk), inc="SV")
                            sv_c = S.v("SV")
                            set_guard(f"ps{par}", "SV")
                            dst = q_l if ocol < 512 else k_l
                            rb = ocol % 512
                            y_store(lambda e, sg=sg, dst=dst, rb=rb, xst=xst:
                                    e.dma_start(
                                        out=dst[rb:rb + 128,
                                                xst * 512:(xst + 1) * 512],
                                        in_=sg[:, :]), waits=[("SV", sv_c)])
                            G[sgk] = ("SS", S.v("SS"))
                    else:
                        vcol = (wst - 4) * 256
                        for tc4 in range(4):
                            par = cur["ps03"] % 4
                            cur["ps03"] += 1
                            pq = ps_[par]
                            for hc in range(HCH):
                                w = []
                                if hc == 0:
                                    w = [("SW", ww), ("SW", xw)] + guard(f"ps{par}")
                                S.op("t", lambda e, pq=pq, aslot=aslot, hc=hc,
                                     tc4=tc4, slot=slot: e.matmul(
                                         pq[:, 0:256],
                                         wp_[:, slot + hc * 512 + tc4 * 128:
                                             slot + hc * 512 + tc4 * 128 + 128],
                                         act_[:, aslot + hc * 256:
                                              aslot + hc * 256 + 256],
                                         start=(hc == 0), stop=False), waits=w)
                            S.op("t", lambda e, pq=pq, vcol=vcol: e.matmul(
                                pq[:, 0:256], onesr_[0:1, 0:128],
                                bias4_[0:1, 1024 + vcol: 1024 + vcol + 256],
                                start=False, stop=True), inc="SP")
                            sp_q = S.v("SP")
                            if tc4 == 3:
                                G[f"aw{aslot}"] = ("SP", sp_q)
                            sg, sgk = next_stg()
                            S.op("v", lambda e, sg=sg, pq=pq: e.tensor_copy(
                                sg[:, 0:256], pq[:, 0:256]),
                                waits=[("SP", sp_q)] + guard(sgk), inc="SV")
                            sv_c = S.v("SV")
                            set_guard(f"ps{par}", "SV")
                            y_store(lambda e, sg=sg, xst=xst, tc4=tc4, vcol=vcol:
                                    e.dma_start(
                                        out=v_l[xst * 512 + tc4 * 128:
                                                xst * 512 + tc4 * 128 + 128,
                                                vcol:vcol + 256],
                                        in_=sg[:, 0:256]), waits=[("SV", sv_c)])
                            G[sgk] = ("SS", S.v("SS"))
            sp_qkv = S.v("SP")
            ss_qkv = S.v("SS")
            G["qs8192"] = ("SP", sp_qkv)
            G["qs12288"] = ("SP", sp_qkv)
            G["qsl0"] = ("SP", sp_qkv)
            G["qsl4096"] = ("SP", sp_qkv)
            wo_w = [None] * 4

            def load_wout(st):
                qs = (st % 2) * 8192
                wo_w[st] = g_dma(lambda e, qs=qs, st=st: e.dma_start(
                    out=wp_[:, qs:qs + 8192].rearrange("p (c f) -> p c f", f=512),
                    in_=wo_t[:, st * 512:(st + 1) * 512].rearrange(
                        "(c p) f -> p c f", p=128)),
                    waits=guard(f"wo{qs}"))
            for i in range(HL):
                slot = (i % 2) * 8192
                kv = xn_[:, slot:slot + 8192]
                w0 = [("SS", ss_qkv)] + guard("xn_r") + guard(f"kv{i % 2}")
                g_dma(lambda e, i=i, kv=kv: e.dma_start(
                    out=kv[:, 0:4096], in_=k_l[i * 128:(i + 1) * 128, :]),
                    waits=w0)
                kv_rdy = g_dma(lambda e, i=i, kv=kv: e.dma_start(
                    out=kv[:, 4096:8192].rearrange("p (c d) -> p c d", d=128),
                    in_=bass.AP(v_l, i * 128,
                                [[512, 128], [512 * 128, 32], [1, 128]])),
                    waits=[])
                qsl = (i % 2) * 4096
                qcache = wp_[:, qsl:qsl + 4096]
                qw = g_dma(lambda e, i=i, qcache=qcache: e.dma_start(
                    out=qcache[:, :], in_=q_l[i * 128:(i + 1) * 128, :]),
                    waits=guard(f"qsl{qsl}"))
                for qi in range(NQB):
                    nkb = 2 * (qi + 1)
                    sp_of = {}
                    exp_done = {}
                    mask_done = {}
                    ptv = {}

                    def emit_S(kb, qi=qi, kv=kv, qw=qw, kv_rdy=kv_rdy,
                               sp_of=sp_of, qcache=qcache):
                        for kc in range(2):
                            par = 2 * (kb % 2) + kc
                            w = []
                            if kc == 0:
                                w = [("SW", kv_rdy), ("SW", qw)] + \
                                    guard(f"ps{2 * (kb % 2)}") + \
                                    guard(f"ps{2 * (kb % 2) + 1}")
                            S.op("t", lambda e, par=par, kb=kb, kc=kc, kv=kv,
                                 qi=qi, qcache=qcache: e.matmul(
                                     ps_[par][:, :],
                                     kv[:, kb * 256 + kc * 128:
                                        kb * 256 + kc * 128 + 128],
                                     qcache[:, qi * 512:(qi + 1) * 512],
                                     start=True, stop=True),
                                 waits=w, inc="SP" if kc == 1 else None)
                        sp_of[kb] = S.v("SP")

                    def emit_exp_mask(kb, qi=qi, sp_of=sp_of, exp_done=exp_done,
                                      mask_done=mask_done, ptv=ptv):
                        pt = act_[:, 4096 + (kb % 2) * 1024:
                                  4096 + (kb % 2) * 1024 + 1024]
                        for kc in range(2):
                            par = 2 * (kb % 2) + kc
                            w = []
                            if kc == 0:
                                w = [("SP", sp_of[kb])] + guard(f"pt{kb % 2}")
                            S.op("s", lambda e, pt=pt, kc=kc, par=par: e.activation(
                                pt[:, kc * 512:(kc + 1) * 512], ps_[par][:, :],
                                AF.Exp, scale=ISQ),
                                waits=w, inc="SA" if kc == 1 else None)
                        sa_ = S.v("SA")
                        set_guard(f"ps{2 * (kb % 2)}", "SA")
                        set_guard(f"ps{2 * (kb % 2) + 1}", "SA")
                        exp_done[kb] = sa_
                        if kb >= 2 * qi:
                            j = kb - 2 * qi
                            pm = act_[:, 6144 + (kb % 2) * 1024:
                                      6144 + (kb % 2) * 1024 + 1024]
                            for kc in range(2):
                                w = [("SA", sa_)]
                                if kc == 0:
                                    w = w + guard(f"pm{kb % 2}")
                                S.op("v", lambda e, pm=pm, pt=pt, j=j, kc=kc:
                                     e.tensor_mul(pm[:, kc * 512:(kc + 1) * 512],
                                                  pt[:, kc * 512:(kc + 1) * 512],
                                                  dm_[:, (2 * j + kc) * 512:
                                                      (2 * j + kc + 1) * 512]),
                                     waits=w, inc="SV" if kc == 1 else None)
                            mask_done[kb] = S.v("SV")
                            ptv[kb] = pm
                        else:
                            mask_done[kb] = 0
                            ptv[kb] = pt

                    cb = [ps_[4], psB_][qi % 2]
                    bb = [psB_, ps_[4]][qi % 2]
                    cbk = ["ps4a", "psBa"][qi % 2]
                    bbk = ["psBa", "ps4a"][qi % 2]

                    def emit_ctx(kb, nkb=nkb, kv=kv, exp_done=exp_done,
                                 mask_done=mask_done, ptv=ptv, cb=cb, cbk=cbk):
                        first = kb == 0
                        last = kb == nkb - 1
                        pt = ptv[kb]
                        w = ([("SV", mask_done[kb])] if mask_done[kb]
                             else [("SA", exp_done[kb])])
                        if first:
                            w = w + guard(cbk) + guard("ps5a")
                        for kc in range(2):
                            S.op("t", lambda e, kb=kb, kc=kc, kv=kv, pt=pt, cb=cb:
                                 e.matmul(
                                     cb[:, :],
                                     kv[:, 4096 + (kb * 2 + kc) * 128:
                                        4096 + (kb * 2 + kc) * 128 + 128],
                                     pt[:, kc * 512:(kc + 1) * 512],
                                     start=(first and kc == 0),
                                     stop=(last and kc == 1)),
                                 waits=w if kc == 0 else [])
                            S.op("t", lambda e, kc=kc, pt=pt: e.matmul(
                                ps_[5][0:1, :], onesc_[:, 0:1],
                                pt[:, kc * 512:(kc + 1) * 512],
                                start=(first and kc == 0), stop=(last and kc == 1)),
                                inc="SP" if kc == 1 else None)
                        G[f"pt{kb % 2}"] = ("SP", S.v("SP"))
                        if mask_done[kb]:
                            G[f"pm{kb % 2}"] = ("SP", S.v("SP"))

                    emit_S(0)
                    for kb in range(nkb):
                        if kb + 1 < nkb:
                            emit_S(kb + 1)
                        emit_exp_mask(kb)
                        emit_ctx(kb)
                    sp_ctx = S.v("SP")
                    cf = ctxf_
                    S.op("v", lambda e, cf=cf, cb=cb: e.tensor_copy(
                        cf[:, :], cb[:, :]),
                         waits=[("SP", sp_ctx)] + guard("cf"), inc="SV")
                    set_guard(cbk, "SV")
                    S.op("v", lambda e: e.tensor_copy(rec_[0:1, :], ps_[5][0:1, :]),
                         waits=guard("recr"), inc="SV")
                    sv_den = S.v("SV")
                    set_guard("ps5a", "SV")
                    S.op("t", lambda e, bb=bb: e.matmul(bb[:, :], onesf_[:, :],
                                                 rec_[0:1, :], start=True, stop=True),
                         waits=[("SV", sv_den)] + guard(bbk), inc="SP")
                    sp_bc = S.v("SP")
                    S.op("v", lambda e, bb=bb: e.reciprocal(rec_[:, :], bb[:, :]),
                         waits=[("SP", sp_bc), ("SV", S.v("SV"))], inc="SV")
                    set_guard(bbk, "SV")
                    sg, sgk = next_stg()
                    S.op("v", lambda e, sg=sg, cf=cf: e.scalar_tensor_tensor(
                        sg[:, :], cf[:, :], 1.0, rec_[:, :], ALU.mult, ALU.mult),
                        waits=guard(sgk) + [("SV", S.v("SV"))], inc="SV")
                    sv_n = S.v("SV")
                    G["cf"] = ("SV", sv_n)
                    set_guard("recr", "SV")
                    rb = (qi // 2) * 512 + i * 128
                    y_store(lambda e, sg=sg, rb=rb, qi=qi: e.dma_start(
                        out=ctx_c[rb:rb + 128, (qi % 2) * 512:(qi % 2) * 512 + 512],
                        in_=sg[:, :]), waits=[("SV", sv_n)])
                    G[sgk] = ("SS", S.v("SS"))
                G[f"kv{i % 2}"] = ("SP", S.v("SP"))
                G[f"qsl{(i % 2) * 4096}"] = ("SP", S.v("SP"))
            set_guard("xn_r", "SP")
            set_guard("act_r", "SP")
            G["wo0"] = ("SP", S.v("SP"))
            G["wo8192"] = ("SP", S.v("SP"))
            load_wout(0)
            load_wout(1)
            ss_ctx = S.v("SS")
            S.op("g", lambda e: e.collective_compute(
                "AllGather", ALU.bypass, replica_groups=[list(range(NCORE))],
                ins=[ctx_c.ap()], outs=[ctx_x.ap()]),
                waits=[("SS", ss_ctx)], inc="SC")
            sc_ctx = S.v("SC")
            ctx_rdy = 0
            for gg in range(4):
                w = [("SC", sc_ctx)]
                if gg == 0:
                    w = w + guard("fm_r")

                def cload(e, gg=gg):
                    if gg == 0:
                        e.reg_load(S.ygreg, meta_[0:1, 12:13])
                    else:
                        e.reg_add(S.ygreg, S.ygreg, 2 * TC * TC)
                    return e.dma_start(
                        out=fm_[:, gg * 4 * TC:(gg + 1) * 4 * TC].rearrange(
                            "p (m t) -> p m t", t=TC),
                        in_=bass.AP(ctx_x, S.ygreg,
                                    [[TC, 128], [128 * TC, 4], [1, TC]]))
                ctx_rdy = S.op("y", cload, waits=w, inc="SY", inc_n=16)
            for st in range(4):
                qs = (st % 2) * 8192
                for tc in range(NT):
                    par = cur["ps03"] % 4
                    cur["ps03"] += 1
                    pq = ps_[par]
                    for hc in range(HCH):
                        w = []
                        if hc == 0:
                            w = [("SW", wo_w[st]), ("SY", ctx_rdy)] + \
                                guard(f"ps{par}")
                        S.op("t", lambda e, pq=pq, qs=qs, hc=hc, tc=tc: e.matmul(
                            pq[:, :],
                            fm_[:, hc * TC + tc * 128: hc * TC + tc * 128 + 128],
                            wp_[:, qs + hc * 512: qs + hc * 512 + 512],
                            start=(hc == 0), stop=False), waits=w)
                    S.op("t", lambda e, pq=pq, st=st: e.matmul(
                        pq[:, :], onesr_[0:1, 0:128],
                        bias4_[0:1, 1536 + st * 512: 1536 + (st + 1) * 512],
                        start=False, stop=True), inc="SP")
                    sp_o = S.v("SP")
                    if tc == NT - 1:
                        G[f"wo{qs}"] = ("SP", sp_o)
                        if st + 2 < 4:
                            load_wout(st + 2)
                    S.op("v", lambda e, tc=tc, st=st, pq=pq: e.tensor_add(
                        r_[:, tc * D + st * 512: tc * D + st * 512 + 512],
                        pq[:, :],
                        r_[:, tc * D + st * 512: tc * D + st * 512 + 512]),
                        waits=[("SP", sp_o)], inc="SV")
                    set_guard(f"ps{par}", "SV")
            set_guard("r", "SV")
            set_guard("fm_r", "SP")
            merge_guard("gu0", "gu0", "wo0")
            merge_guard("gu2048", "gu2048", "wo0")
            merge_guard("gu4096", "gu4096", "wo0")
            merge_guard("gu6144", "gu6144", "wo0")
            merge_guard("ds8192", "ds8192", "qs8192", "qs12288", "wo8192")
            merge_guard("ds13824", "ds13824", "qs12288", "wo8192")
            merge_guard("ps4", "ps4", "ps4a")
            merge_guard("ps5", "ps5", "ps5a")

        if "id" in stages:
            pass
        if "ln1" in stages:
            svf = emit_ln()
            for tc in range(NT):
                S.op("v", lambda e, tc=tc: e.tensor_copy(
                    r_[:, tc * D:(tc + 1) * D], xn_[:, tc * D:(tc + 1) * D]),
                    waits=[("SV", svf)], inc="SV")
            set_guard("r", "SV")
        if "hcb" in stages:
            emit_ffn(())
            S.op("v", lambda e: e.tensor_copy(r_[:, 0:512], scr_[:, :]), inc="SV")
            set_guard("r", "SV")
        if "ffn1" in stages:
            emit_ffn(experts)
        if "attn" in stages:
            emit_attn()
        if "ffn2" in stages:
            emit_ffn(experts)

        rg = G.get("r")
        y_store(lambda e: e.dma_start(
            out=y_t.ap().rearrange("(c p) f -> p c f", p=128),
            in_=r_[:, :].rearrange("p (c f) -> p c f", f=D)),
            waits=[rg] if rg else [])
        S.op("y", None, waits=[("SS", S.v("SS"))])

        with nc.Block() as block:
            @block.sync
            def _(e):
                with e.register("dynoffy") as reg:
                    S.ygreg = reg
                    S.run("y", e)

            @block.gpsimd
            def _(e):
                with e.register("dynoff") as reg:
                    S.greg = reg
                    S.run("g", e)

            @block.tensor
            def _(e):
                S.run("t", e)

            @block.vector
            def _(e):
                S.run("v", e)

            @block.scalar
            def _(e):
                S.run("s", e)

    return nc


def _host_prep(inputs):
    x = np.ascontiguousarray(np.asarray(inputs["L0_attn_x"], dtype=np.float32))
    wqkv = np.asarray(inputs["L0_attn_w_qkv"], np.float32)
    bqkv = np.asarray(inputs["L0_attn_b_qkv"], np.float32)
    wout = np.ascontiguousarray(np.asarray(inputs["L0_attn_w_out"], np.float32))
    bout = np.asarray(inputs["L0_attn_b_out"], np.float32)
    gate = np.asarray(inputs["L0_ffn_gate_weight"], np.float32)
    guw = np.asarray(inputs["L0_ffn_gate_up_weights"], np.float32)
    dww = np.asarray(inputs["L0_ffn_down_weights"], np.float32)
    sgu = np.asarray(inputs["L0_ffn_shared_gate_up"], np.float32)
    sdw = np.asarray(inputs["L0_ffn_shared_down"], np.float32)

    w3 = wqkv.reshape(D, NH, 3, 128)
    b3 = bqkv.reshape(NH, 3, 128)
    iden = np.eye(128, dtype=ml_dtypes.bfloat16)
    p = np.arange(128)[:, None]
    q = np.arange(512)[None, :]
    dmask = np.stack([(128 * i + p <= q) for i in range(4)], axis=1)
    dmask = np.ascontiguousarray(dmask.astype(ml_dtypes.bfloat16).reshape(128, 2048))

    xt = x.reshape(NCORE * TC, D)
    shared = dict(
        wo=wout,
        bo=np.ascontiguousarray(bout.reshape(1, D)),
        gateT=np.ascontiguousarray(gate.T),
        guT=np.ascontiguousarray(guw.transpose(0, 2, 1)),
        dT=np.ascontiguousarray(dww.transpose(0, 2, 1)),
        sguT=np.ascontiguousarray(sgu.T),
        sdT=np.ascontiguousarray(sdw.T),
        iden=iden, dmask=dmask)
    in_maps = []
    for c in range(NCORE):
        s = c // 4
        g4 = c % 4
        heads = [4 * g4 + i for i in range(HL)]
        wql = np.concatenate(
            [np.ascontiguousarray(w3[:, heads, k, :].reshape(D, 512))
             for k in range(3)], axis=1)
        bql = np.concatenate(
            [b3[heads, k, :].reshape(1, 512) for k in range(3)], axis=1)
        meta = np.zeros((1, 16), np.int32)
        meta[0, 12] = (4 * s * 2 * TC + g4 * 512) * TC
        meta[0, 13] = s * 4 * 2 * TC * TC
        m = dict(shared)
        m["x"] = np.ascontiguousarray(xt[TC * c: TC * (c + 1)])
        m["wqkvl"] = np.ascontiguousarray(wql)
        m["bqkvl"] = np.ascontiguousarray(bql)
        m["meta"] = meta
        in_maps.append(m)
    return in_maps


def kernel(**inputs):
    in_maps = _host_prep(inputs)
    nc = build_nc()
    res = run_bass_kernel_spmd(nc, in_maps, core_ids=list(range(NCORE)))
    out = np.concatenate([res.results[c]["y"] for c in range(NCORE)], axis=0)
    return out.reshape(2, SEQ, D)


if __name__ == "__main__":
    nc = build_nc()
    print("built ok; instructions:", nc.n_instructions())


# version 09
# speedup vs baseline: 1.0269x; 1.0269x over previous
import numpy as np
import ml_dtypes
from contextlib import ExitStack

import concourse.bass as bass
import concourse.mybir as mybir
from concourse.bass_utils import run_bass_kernel_spmd

F32 = mybir.dt.float32
BF16 = mybir.dt.bfloat16
I32 = mybir.dt.int32
AF = mybir.ActivationFunctionType
ALU = mybir.AluOpType
AX = mybir.AxisListType.X

NCORE = 8
D = 2048
TC = 1024
NT = 8
HCH = 16
FF = 1408
NFC = 11
NEXP = 10
NH = 16
HL = 4
SEQ = 4096
NQB = 8
ISQ = 0.08838834764831845

GU_SLOTS = [0, 2048, 4096, 6144]
D_SLOTS = [8192, 13824]
QS_SLOTS = [8192, 12288]
WP_ELS = 19456


class Sched:
    ENGINES = ("t", "v", "s", "g", "y")

    def __init__(self):
        self.q = {k: [] for k in self.ENGINES}
        self.cnt = {k: 0 for k in ("SW", "SY", "SS", "SP", "SV", "SA", "SG",
                                   "SC", "SC2")}
        self.sems = {}
        self.greg = None

    def op(self, eng, fn, waits=(), inc=None, inc_n=1):
        ws = tuple((s, v) for (s, v) in waits if s is not None and v and v > 0)
        if inc is not None:
            self.cnt[inc] += inc_n
        self.q[eng].append((ws, fn, inc, inc_n))
        return self.cnt[inc] if inc is not None else None

    def v(self, sem):
        return self.cnt[sem]

    def run(self, eng, e):
        waited = {}
        for ws, fn, inc, inc_n in self.q[eng]:
            for s, val in ws:
                if val > waited.get(s, 0):
                    e.wait_ge(self.sems[s], val)
                    waited[s] = val
            try:
                inst = fn(e) if fn is not None else None
            except Exception:
                import traceback
                traceback.print_exc()
                raise
            if inc is not None:
                inst.then_inc(self.sems[inc], inc_n)


def build_nc(stages=("ffn1", "attn", "ffn2"), experts=tuple(range(NEXP))):
    nc = bass.Bass("TRN2", target_bir_lowering=False)

    x_t = nc.dram_tensor("x", [TC, D], F32, kind="ExternalInput")
    wqkvl_t = nc.dram_tensor("wqkvl", [D, 1536], F32, kind="ExternalInput")
    bqkvl_t = nc.dram_tensor("bqkvl", [1, 1536], F32, kind="ExternalInput")
    wo_t = nc.dram_tensor("wo", [D, D], F32, kind="ExternalInput")
    bo_t = nc.dram_tensor("bo", [1, D], F32, kind="ExternalInput")
    gateT_t = nc.dram_tensor("gateT", [D, 8], F32, kind="ExternalInput")
    guT_t = nc.dram_tensor("guT", [8, D, 2 * FF], F32, kind="ExternalInput")
    dT_t = nc.dram_tensor("dT", [8, FF, D], F32, kind="ExternalInput")
    sguT_t = nc.dram_tensor("sguT", [D, 4 * FF], F32, kind="ExternalInput")
    sdT_t = nc.dram_tensor("sdT", [2 * FF, D], F32, kind="ExternalInput")
    iden_t = nc.dram_tensor("iden", [128, 128], BF16, kind="ExternalInput")
    dmask_t = nc.dram_tensor("dmask", [128, 2048], BF16, kind="ExternalInput")
    meta_t = nc.dram_tensor("meta", [1, 16], I32, kind="ExternalInput")
    y_t = nc.dram_tensor("y", [TC, D], F32, kind="ExternalOutput")

    xn2c = nc.dram_tensor("xn2c", [2 * TC, TC], BF16)
    xn2x = nc.dram_tensor("xn2x", [16 * TC, TC], BF16, addr_space="Shared")
    q_l = nc.dram_tensor("q_l", [512, SEQ], BF16)
    k_l = nc.dram_tensor("k_l", [512, SEQ], BF16)
    v_l = nc.dram_tensor("v_l", [SEQ, 512], BF16)
    ctx_c = nc.dram_tensor("ctx_c", [2 * TC, TC], BF16)
    ctx_x = nc.dram_tensor("ctx_x", [16 * TC, TC], BF16, addr_space="Shared")
    GROUPS = [[0, 1, 2, 3], [4, 5, 6, 7]]

    S = Sched()

    with ExitStack() as ectx:
        e_ = ectx.enter_context
        r_ = e_(nc.sbuf_tensor("r", [128, NT * D], F32))
        xn_ = e_(nc.sbuf_tensor("xn", [128, NT * D], BF16))
        fm_ = e_(nc.sbuf_tensor("fm", [128, HCH * TC], BF16))
        wp_ = e_(nc.sbuf_tensor("wp", [128, WP_ELS], BF16))
        act_ = e_(nc.sbuf_tensor("act", [128, NFC * TC], BF16))
        stg_ = [e_(nc.sbuf_tensor(f"stg{i}", [128, 512], BF16)) for i in range(2)]
        scr_ = e_(nc.sbuf_tensor("scr", [128, 512], F32))
        ctxf_ = e_(nc.sbuf_tensor("ctxf", [128, 512], F32))
        idn_ = e_(nc.sbuf_tensor("idn", [128, 128], BF16))
        dm_ = e_(nc.sbuf_tensor("dm", [128, 2048], BF16))
        gat_ = e_(nc.sbuf_tensor("gat", [128, HCH * 8], BF16))
        onesr_ = e_(nc.sbuf_tensor("onesr", [128, 512], BF16))
        onesc_ = e_(nc.sbuf_tensor("onesc", [128, 1], BF16))
        onesf_ = e_(nc.sbuf_tensor("onesf", [1, 128], F32))
        ceps_ = e_(nc.sbuf_tensor("ceps", [128, 1], F32))
        bias4_ = e_(nc.sbuf_tensor("bias4", [1, 3584], BF16))
        meta_ = e_(nc.sbuf_tensor("metas", [1, 16], I32))
        ps_ = [e_(nc.psum_tensor(f"ps{i}", [128, 512], F32)) for i in range(6)]
        psB_ = e_(nc.psum_tensor("psB", [128, 512], F32))
        psT_ = [e_(nc.psum_tensor("psT0", [128, 1024], BF16))]
        for name in S.cnt:
            S.sems[name] = e_(nc.semaphore(name))

        hcb_ = scr_[:, 0:64]
        rt_ = scr_[:, 64:112]
        lnst_ = scr_[:, 112:144]
        rec_ = scr_

        G = {}

        def guard(name):
            g = G.get(name)
            return [g] if g else []

        def set_guard(name, sem):
            G[name] = (sem, S.v(sem))

        def merge_guard(name, *names):
            best = None
            for n in names:
                g = G.get(n)
                if g and (best is None or g[1] > best[1]):
                    best = g
            if best:
                G[name] = best

        def g_dma(fn, waits=()):
            return S.op("g", fn, waits=waits, inc="SW", inc_n=16)

        def y_load(fn, waits=()):
            return S.op("y", fn, waits=waits, inc="SY", inc_n=16)

        def y_store(fn, waits=()):
            return S.op("y", fn, waits=waits, inc="SS", inc_n=16)

        for xq in range(4):
            y_load(lambda e, xq=xq: e.dma_start(
                out=r_[:, xq * 2 * D:(xq + 1) * 2 * D].rearrange(
                    "p (c f) -> p c f", f=D),
                in_=x_t.ap().rearrange("(c p) f -> p c f", p=128)[
                    :, xq * 2:(xq + 1) * 2, :]))
        y_load(lambda e: e.dma_start(out=idn_[:, :], in_=iden_t.ap()))
        y_load(lambda e: e.dma_start(out=dm_[:, :], in_=dmask_t.ap()))
        y_meta = y_load(lambda e: e.dma_start(out=meta_[:, :], in_=meta_t.ap()))
        S.op("g", lambda e: e.memset(onesr_[:, :], 1.0), inc="SG")
        S.op("g", lambda e: e.memset(onesc_[:, :], 1.0), inc="SG")
        S.op("g", lambda e: e.memset(onesf_[:, :], 1.0), inc="SG")
        S.op("g", lambda e: e.memset(ceps_[:, :], 1e-5), inc="SG")
        sg_pre = S.v("SG")
        g_dma(lambda e: e.dma_start(
            out=gat_[:, :].rearrange("p (c e) -> p c e", e=8),
            in_=gateT_t.ap().rearrange("(c p) e -> p c e", p=128)))
        g_dma(lambda e: e.dma_start(out=bias4_[0:1, 0:1536], in_=bqkvl_t.ap()))
        w_pre = g_dma(lambda e: e.dma_start(out=bias4_[0:1, 1536:3584], in_=bo_t.ap()))
        pre_t = [("SY", 112), ("SG", sg_pre), ("SW", w_pre)]
        rready = {}

        def rwait(tc):
            w = [("SY", 16 * (tc // 2 + 1)), ("SG", sg_pre), ("SW", w_pre)]
            if tc in rready:
                w.append(("SV", rready[tc]))
            else:
                w = w + guard("r")
            return w

        cur = {"gupair": 0, "dstrip": 0, "qs": 0, "ps03": 0, "stg": 0}

        def next_stg():
            i = cur["stg"] % 2
            cur["stg"] += 1
            return stg_[i], f"stg{i}"

        def emit_ln():
            def half_stats(half):
                base = half * 4
                for tc in range(base, base + 4):
                    rsl = r_[:, tc * D:(tc + 1) * D]
                    dump = xn_[:, tc * D:(tc + 1) * D]
                    w = rwait(tc)
                    if tc == 0:
                        w = w + guard("xn_r")
                    S.op("s", lambda e, o=dump, i=rsl, tc=tc: e.activation(
                        o, i, AF.Square,
                        accum_out=lnst_[:, 4 * tc + 1:4 * tc + 2]),
                        waits=w, inc="SA")
                    S.op("s", lambda e, o=dump, i=rsl, tc=tc: e.activation(
                        o, i, AF.Copy,
                        accum_out=lnst_[:, 4 * tc:4 * tc + 1]), inc="SA")
                sa_sums = S.v("SA")

                def st_(j):
                    return lnst_.rearrange("p (a b) -> p a b", b=4)[
                        :, base:base + 4, j:j + 1]

                def rt3(a, b):
                    return rt_[:, a:b].rearrange("p (a b) -> p a b", b=1)
                ra, rb = 4 * half, 4 * half + 4
                S.op("v", lambda e: e.tensor_scalar(st_(2), st_(0), -1.0 / D, None,
                                                    ALU.mult),
                     waits=[("SA", sa_sums), ("SV", S.v("SV"))], inc="SV")
                sv_negm = S.v("SV")
                S.op("s", lambda e: e.activation(rt3(ra, rb), st_(2), AF.Square),
                     waits=[("SV", sv_negm)], inc="SA")
                sa_msq = S.v("SA")
                S.op("v", lambda e: e.tensor_scalar(rt3(8 + ra, 8 + rb), st_(1),
                                                    1.0 / D, None, ALU.mult),
                     inc="SV")
                S.op("v", lambda e: e.tensor_sub(rt3(16 + ra, 16 + rb),
                                                 rt3(8 + ra, 8 + rb),
                                                 rt3(ra, rb)),
                     waits=[("SA", sa_msq), ("SV", S.v("SV"))], inc="SV")
                sv_var = S.v("SV")
                S.op("s", lambda e: e.activation(rt3(24 + ra, 24 + rb),
                                                 rt3(16 + ra, 16 + rb), AF.Ln,
                                                 bias=ceps_[:, :]),
                     waits=[("SV", sv_var)], inc="SA")
                S.op("s", lambda e: e.activation(st_(3), rt3(24 + ra, 24 + rb),
                                                 AF.Exp, scale=-0.5), inc="SA")
                sa_rstd = S.v("SA")
                for tc in range(base, base + 4):
                    w = [("SA", sa_rstd)]
                    if tc == 0:
                        w = w + guard("xn_r")
                    S.op("v", lambda e, tc=tc: e.tensor_scalar(
                        xn_[:, tc * D:(tc + 1) * D], r_[:, tc * D:(tc + 1) * D],
                        lnst_[:, 4 * tc + 2:4 * tc + 3],
                        lnst_[:, 4 * tc + 3:4 * tc + 4],
                        ALU.add, ALU.mult), waits=w, inc="SV")
                return S.v("SV")

            def half_trans(half, sv_xn):
                base = half * 4
                for hc in range(HCH):
                    pst = psT_[0]
                    for k in range(4):
                        tc = base + k
                        w = []
                        if k == 0:
                            w = [("SV", sv_xn), ("SY", 80)] + guard("psT0")
                            if hc == 0 and half == 0:
                                w = w + guard("fm_r")
                        S.op("t", lambda e, pst=pst, tc=tc, hc=hc, k=k:
                             e.transpose(
                                 pst[:, k * 128:(k + 1) * 128],
                                 xn_[:, tc * D + hc * 128: tc * D + hc * 128 + 128],
                                 idn_[:, :]),
                             waits=w, inc="SP" if k == 3 else None)
                    sp_tr = S.v("SP")
                    w = [("SP", sp_tr)]
                    if hc == 0 and half == 0:
                        w = w + guard("fm_r")
                    S.op("v", lambda e, hc=hc, half=half, pst=pst: e.tensor_copy(
                        fm_[:, hc * TC + half * 512: hc * TC + half * 512 + 512],
                        pst[:, 0:512]), waits=w, inc="SV")
                    set_guard("psT0", "SV")

            sv0 = half_stats(0)
            sv1 = half_stats(1)
            half_trans(0, sv0)
            half_trans(1, sv1)
            set_guard("xn_r", "SP")
            return S.v("SV")

        def emit_ffn(experts=tuple(range(NEXP))):
            sv_fm = emit_ln()
            last_sa_e = 0
            for tc in range(NT):
                psl = ps_[4 + tc % 2]
                for hc in range(HCH):
                    w = []
                    if hc == 0:
                        w = [("SV", sv_fm)] + guard(f"ps{4 + tc % 2}") + pre_t
                    S.op("t", lambda e, psl=psl, hc=hc, tc=tc: e.matmul(
                        psl[:, 0:8],
                        fm_[:, hc * TC + tc * 128: hc * TC + tc * 128 + 128],
                        gat_[:, hc * 8:(hc + 1) * 8],
                        start=(hc == 0), stop=(hc == HCH - 1)),
                        waits=w, inc="SP" if hc == HCH - 1 else None)
                sp_l = S.v("SP")

                def vs(fn, extra=()):
                    S.op("v", fn, waits=[("SV", S.v("SV"))] + list(extra), inc="SV")
                vs(lambda e, psl=psl: e.tensor_copy(rt_[:, 0:8], psl[:, 0:8]),
                   [("SP", sp_l), ("SA", last_sa_e)])
                set_guard(f"ps{4 + tc % 2}", "SV")
                vs(lambda e: e.reduce_max(rt_[:, 8:9], rt_[:, 0:8], axis=AX))
                vs(lambda e: e.tensor_scalar(rt_[:, 13:14], rt_[:, 8:9],
                                             -1.0, None, ALU.mult))
                sv_m1 = S.v("SV")
                S.op("s", lambda e: e.activation(rt_[:, 16:24], rt_[:, 0:8], AF.Exp,
                                                 bias=rt_[:, 13:14]),
                     waits=[("SV", sv_m1)], inc="SA")
                last_sa_e = S.v("SA")
                vs(lambda e: e.tensor_scalar(rt_[:, 24:32], rt_[:, 0:8],
                                             rt_[:, 8:9], None, ALU.is_ge))
                vs(lambda e: e.scalar_tensor_tensor(
                    rt_[:, 32:40], rt_[:, 24:32], -1e30, rt_[:, 0:8],
                    ALU.mult, ALU.add))
                vs(lambda e: e.reduce_max(rt_[:, 9:10], rt_[:, 32:40], axis=AX))
                vs(lambda e: e.tensor_scalar(rt_[:, 24:32], rt_[:, 0:8],
                                             rt_[:, 9:10], None, ALU.is_ge),
                   [("SA", last_sa_e)])
                vs(lambda e: e.reduce_sum(rt_[:, 10:11], rt_[:, 16:24], axis=AX))
                vs(lambda e: e.reciprocal(rt_[:, 11:12], rt_[:, 10:11]))
                vs(lambda e: e.tensor_scalar(rt_[:, 12:13], rt_[:, 11:12],
                                             0.5, None, ALU.mult))
                vs(lambda e, tc=tc: e.scalar_tensor_tensor(
                    hcb_[:, tc * 8:(tc + 1) * 8], rt_[:, 16:24], rt_[:, 12:13],
                    rt_[:, 24:32], ALU.mult, ALU.mult))
            loads = []
            for ex in experts:
                loads += [("gu", ex, fc) for fc in range(NFC)]
                loads += [("d", ex, hs) for hs in range(4)]
            linfo = {}

            def emit_load(item):
                kind, ex, j = item
                if kind == "gu":
                    k = cur["gupair"]
                    cur["gupair"] += 1
                    s1 = GU_SLOTS[(2 * k) % 4]
                    s2 = GU_SLOTS[(2 * k) % 4 + 1]
                    if ex < 8:
                        src1 = guT_t[ex, :, j * 128:(j + 1) * 128]
                        src2 = guT_t[ex, :, FF + j * 128: FF + j * 128 + 128]
                    else:
                        kk = ex - 8
                        src1 = sguT_t[:, kk * FF + j * 128: kk * FF + j * 128 + 128]
                        src2 = sguT_t[:, 2 * FF + kk * FF + j * 128:
                                      2 * FF + kk * FF + j * 128 + 128]
                    w1 = g_dma(lambda e, s=s1, src=src1: e.dma_start(
                        out=wp_[:, s:s + 2048].rearrange("p (c f) -> p c f", f=128),
                        in_=src.rearrange("(c p) f -> p c f", p=128)),
                        waits=guard(f"gu{s1}"))
                    w2 = g_dma(lambda e, s=s2, src=src2: e.dma_start(
                        out=wp_[:, s:s + 2048].rearrange("p (c f) -> p c f", f=128),
                        in_=src.rearrange("(c p) f -> p c f", p=128)),
                        waits=guard(f"gu{s2}"))
                    linfo[item] = (w1, w2, s1, s2)
                else:
                    ds = D_SLOTS[cur["dstrip"] % 2]
                    cur["dstrip"] += 1
                    if ex < 8:
                        src = dT_t[ex, :, j * 512:(j + 1) * 512]
                    else:
                        kk = ex - 8
                        src = sdT_t[kk * FF:(kk + 1) * FF, j * 512:(j + 1) * 512]
                    wd = g_dma(lambda e, ds=ds, src=src: e.dma_start(
                        out=wp_[:, ds:ds + 5632].rearrange("p (c h) -> p c h", h=512),
                        in_=src.rearrange("(c p) h -> p c h", p=128)),
                        waits=guard(f"ds{ds}"))
                    linfo[item] = (wd, ds)

            pumped = [0]

            def pump(n):
                while pumped[0] < min(n, len(loads)):
                    emit_load(loads[pumped[0]])
                    pumped[0] += 1

            pump(2)
            ui = 0
            for ex in experts:
                for fc in range(NFC):
                    w1, w2, s1, s2 = linfo[("gu", ex, fc)]
                    for th in range(2):
                        par = (fc * 2 + th) % 2
                        pa, pb = ps_[2 * par], ps_[2 * par + 1]
                        for hc in range(HCH):
                            w = []
                            if hc == 0:
                                w = [("SW", w1), ("SV", sv_fm)] + guard(f"ps{2 * par}")
                            S.op("t", lambda e, pa=pa, s1=s1, hc=hc, th=th: e.matmul(
                                pa[:, :], wp_[:, s1 + hc * 128: s1 + hc * 128 + 128],
                                fm_[:, hc * TC + th * 512: hc * TC + th * 512 + 512],
                                start=(hc == 0), stop=(hc == HCH - 1)), waits=w)
                        for hc in range(HCH):
                            w = []
                            if hc == 0:
                                w = [("SW", w2)] + guard(f"ps{2 * par + 1}")
                            S.op("t", lambda e, pb=pb, s2=s2, hc=hc, th=th: e.matmul(
                                pb[:, :], wp_[:, s2 + hc * 128: s2 + hc * 128 + 128],
                                fm_[:, hc * TC + th * 512: hc * TC + th * 512 + 512],
                                start=(hc == 0), stop=(hc == HCH - 1)),
                                waits=w, inc="SP" if hc == HCH - 1 else None)
                        sp_gu = S.v("SP")
                        if th == 1:
                            G[f"gu{s1}"] = ("SP", sp_gu)
                            G[f"gu{s2}"] = ("SP", sp_gu)
                        sg, sgk = next_stg()
                        S.op("s", lambda e, sg=sg, pa=pa: e.activation(
                            sg[:, :], pa[:, :], AF.Silu),
                            waits=[("SP", sp_gu)] + guard(sgk), inc="SA")
                        sa_si = S.v("SA")
                        set_guard(f"ps{2 * par}", "SA")
                        aw = [("SA", sa_si)]
                        if fc == 0 and th == 0:
                            aw = aw + guard("act_r")
                        S.op("v", lambda e, fc=fc, th=th, sg=sg, pb=pb: e.tensor_mul(
                            act_[:, fc * TC + th * 512: fc * TC + th * 512 + 512],
                            sg[:, :], pb[:, :]), waits=aw, inc="SV")
                        set_guard(f"ps{2 * par + 1}", "SV")
                        G[sgk] = ("SV", S.v("SV"))
                    ui += 1
                    pump(ui + 2)
                sv_act = S.v("SV")
                for hs in range(4):
                    wd, ds = linfo[("d", ex, hs)]
                    for tc in range(NT):
                        par = 4 + (hs * NT + tc) % 2
                        pd = ps_[par]
                        for fc in range(NFC):
                            w = []
                            if fc == 0:
                                w = [("SW", wd), ("SV", sv_act)] + guard(f"ps{par}")
                            S.op("t", lambda e, pd=pd, fc=fc, tc=tc, ds=ds: e.matmul(
                                pd[:, :],
                                act_[:, fc * TC + tc * 128: fc * TC + tc * 128 + 128],
                                wp_[:, ds + fc * 512: ds + fc * 512 + 512],
                                start=(fc == 0), stop=(fc == NFC - 1)),
                                waits=w, inc="SP" if fc == NFC - 1 else None)
                        sp_d = S.v("SP")
                        if tc == NT - 1:
                            G[f"ds{ds}"] = ("SP", sp_d)
                        scal = (hcb_[:, tc * 8 + ex: tc * 8 + ex + 1]
                                if ex < 8 else 0.5)
                        S.op("v", lambda e, tc=tc, hs=hs, pd=pd, scal=scal:
                             e.scalar_tensor_tensor(
                                 r_[:, tc * D + hs * 512: tc * D + hs * 512 + 512],
                                 pd[:, :], scal,
                                 r_[:, tc * D + hs * 512: tc * D + hs * 512 + 512],
                                 ALU.mult, ALU.add),
                             waits=[("SP", sp_d)], inc="SV")
                        if experts and ex == experts[-1] and hs == 3:
                            rready[tc] = S.v("SV")
                        set_guard(f"ps{par}", "SV")
                    ui += 1
                    pump(ui + 2)
            set_guard("r", "SV")
            set_guard("fm_r", "SP")
            set_guard("act_r", "SP")

        def emit_attn():
            sv_fm = emit_ln()
            merge_guard("wpA", "gu0", "gu2048", "gu4096", "gu6144")
            merge_guard("wpB", "ds8192", "ds13824", "qs8192", "qs12288")
            sxc = y_store(lambda e: e.dma_start(
                out=xn2c.ap().rearrange("(c p) t -> p c t", p=128),
                in_=fm_[:, :].rearrange("p (c t) -> p c t", t=TC)),
                waits=[("SV", sv_fm)])
            S.op("g", lambda e: e.collective_compute(
                "AllGather", ALU.bypass, replica_groups=[list(range(NCORE))],
                ins=[xn2c.ap()], outs=[xn2x.ap()]),
                waits=[("SS", sxc)], inc="SC")
            sc_xn = S.v("SC")
            for xst in range(8):
                slot = (xst % 2) * 8192

                def xload(e, slot=slot, xst=xst):
                    e.reg_load(S.greg, meta_[0:1, 13:14])
                    e.reg_add(S.greg, S.greg,
                              (xst // 2) * 2 * TC * TC + (xst % 2) * 512)
                    return e.dma_start(
                        out=wp_[:, slot:slot + 8192].rearrange(
                            "p (c t) -> p c t", t=512),
                        in_=bass.AP(xn2x, S.greg,
                                    [[TC, 128], [128 * TC, 16], [1, 512]]))
                xw = g_dma(xload, waits=[("SC", sc_xn), ("SY", 112)] +
                           guard("wpA" if slot == 0 else "wpB"))
                for wst in range(6):
                    aslot = (wst % 2) * 4096
                    ww = g_dma(lambda e, aslot=aslot, wst=wst: e.dma_start(
                        out=act_[:, aslot:aslot + 4096].rearrange(
                            "p (c f) -> p c f", f=256),
                        in_=wqkvl_t[:, wst * 256:(wst + 1) * 256].rearrange(
                            "(c p) f -> p c f", p=128)),
                        waits=guard(f"aw{aslot}") +
                        (guard("act_r") if xst == 0 and wst < 2 else []))
                    if wst < 4:
                        for oc in range(2):
                            ocol = wst * 256 + oc * 128
                            par = cur["ps03"] % 4
                            cur["ps03"] += 1
                            pq = ps_[par]
                            for hc in range(HCH):
                                w = []
                                if hc == 0:
                                    w = [("SW", ww), ("SW", xw)] + guard(f"ps{par}")
                                S.op("t", lambda e, pq=pq, aslot=aslot, hc=hc,
                                     oc=oc, slot=slot: e.matmul(
                                         pq[:, :],
                                         act_[:, aslot + hc * 256 + oc * 128:
                                              aslot + hc * 256 + oc * 128 + 128],
                                         wp_[:, slot + hc * 512:
                                             slot + hc * 512 + 512],
                                         start=(hc == 0), stop=False), waits=w)
                            S.op("t", lambda e, pq=pq, ocol=ocol: e.matmul(
                                pq[:, :], bias4_[0:1, ocol:ocol + 128],
                                onesr_[0:1, 0:512], start=False, stop=True),
                                inc="SP")
                            sp_q = S.v("SP")
                            if oc == 1:
                                G[f"aw{aslot}"] = ("SP", sp_q)
                            sg, sgk = next_stg()
                            S.op("v", lambda e, sg=sg, pq=pq: e.tensor_copy(
                                sg[:, :], pq[:, :]),
                                waits=[("SP", sp_q)] + guard(sgk), inc="SV")
                            sv_c = S.v("SV")
                            set_guard(f"ps{par}", "SV")
                            dst = q_l if ocol < 512 else k_l
                            rb = ocol % 512
                            y_store(lambda e, sg=sg, dst=dst, rb=rb, xst=xst:
                                    e.dma_start(
                                        out=dst[rb:rb + 128,
                                                xst * 512:(xst + 1) * 512],
                                        in_=sg[:, :]), waits=[("SV", sv_c)])
                            G[sgk] = ("SS", S.v("SS"))
                    else:
                        vcol = (wst - 4) * 256
                        for tc4 in range(4):
                            par = cur["ps03"] % 4
                            cur["ps03"] += 1
                            pq = ps_[par]
                            for hc in range(HCH):
                                w = []
                                if hc == 0:
                                    w = [("SW", ww), ("SW", xw)] + guard(f"ps{par}")
                                S.op("t", lambda e, pq=pq, aslot=aslot, hc=hc,
                                     tc4=tc4, slot=slot: e.matmul(
                                         pq[:, 0:256],
                                         wp_[:, slot + hc * 512 + tc4 * 128:
                                             slot + hc * 512 + tc4 * 128 + 128],
                                         act_[:, aslot + hc * 256:
                                              aslot + hc * 256 + 256],
                                         start=(hc == 0), stop=False), waits=w)
                            S.op("t", lambda e, pq=pq, vcol=vcol: e.matmul(
                                pq[:, 0:256], onesr_[0:1, 0:128],
                                bias4_[0:1, 1024 + vcol: 1024 + vcol + 256],
                                start=False, stop=True), inc="SP")
                            sp_q = S.v("SP")
                            if tc4 == 3:
                                G[f"aw{aslot}"] = ("SP", sp_q)
                            sg, sgk = next_stg()
                            S.op("v", lambda e, sg=sg, pq=pq: e.tensor_copy(
                                sg[:, 0:256], pq[:, 0:256]),
                                waits=[("SP", sp_q)] + guard(sgk), inc="SV")
                            sv_c = S.v("SV")
                            set_guard(f"ps{par}", "SV")
                            y_store(lambda e, sg=sg, xst=xst, tc4=tc4, vcol=vcol:
                                    e.dma_start(
                                        out=v_l[xst * 512 + tc4 * 128:
                                                xst * 512 + tc4 * 128 + 128,
                                                vcol:vcol + 256],
                                        in_=sg[:, 0:256]), waits=[("SV", sv_c)])
                            G[sgk] = ("SS", S.v("SS"))
            sp_qkv = S.v("SP")
            ss_qkv = S.v("SS")
            G["qs8192"] = ("SP", sp_qkv)
            G["qs12288"] = ("SP", sp_qkv)
            G["qsl0"] = ("SP", sp_qkv)
            G["qsl4096"] = ("SP", sp_qkv)
            wo_w = [None] * 4

            def load_wout(st):
                qs = (st % 2) * 8192
                wo_w[st] = g_dma(lambda e, qs=qs, st=st: e.dma_start(
                    out=wp_[:, qs:qs + 8192].rearrange("p (c f) -> p c f", f=512),
                    in_=wo_t[:, st * 512:(st + 1) * 512].rearrange(
                        "(c p) f -> p c f", p=128)),
                    waits=guard(f"wo{qs}"))
            for i in range(HL):
                slot = (i % 2) * 8192
                kv = xn_[:, slot:slot + 8192]
                w0 = [("SS", ss_qkv)] + guard("xn_r") + guard(f"kv{i % 2}")
                g_dma(lambda e, i=i, kv=kv: e.dma_start(
                    out=kv[:, 0:4096], in_=k_l[i * 128:(i + 1) * 128, :]),
                    waits=w0)
                kv_rdy = g_dma(lambda e, i=i, kv=kv: e.dma_start(
                    out=kv[:, 4096:8192].rearrange("p (c d) -> p c d", d=128),
                    in_=bass.AP(v_l, i * 128,
                                [[512, 128], [512 * 128, 32], [1, 128]])),
                    waits=[])
                qsl = (i % 2) * 4096
                qcache = wp_[:, qsl:qsl + 4096]
                qw = g_dma(lambda e, i=i, qcache=qcache: e.dma_start(
                    out=qcache[:, :], in_=q_l[i * 128:(i + 1) * 128, :]),
                    waits=guard(f"qsl{qsl}"))
                for qi in range(NQB):
                    nkb = 2 * (qi + 1)
                    sp_of = {}
                    exp_done = {}
                    mask_done = {}
                    ptv = {}

                    def emit_S(kb, qi=qi, kv=kv, qw=qw, kv_rdy=kv_rdy,
                               sp_of=sp_of, qcache=qcache):
                        for kc in range(2):
                            par = 2 * (kb % 2) + kc
                            w = []
                            if kc == 0:
                                w = [("SW", kv_rdy), ("SW", qw)] + \
                                    guard(f"ps{2 * (kb % 2)}") + \
                                    guard(f"ps{2 * (kb % 2) + 1}")
                            S.op("t", lambda e, par=par, kb=kb, kc=kc, kv=kv,
                                 qi=qi, qcache=qcache: e.matmul(
                                     ps_[par][:, :],
                                     kv[:, kb * 256 + kc * 128:
                                        kb * 256 + kc * 128 + 128],
                                     qcache[:, qi * 512:(qi + 1) * 512],
                                     start=True, stop=True),
                                 waits=w, inc="SP" if kc == 1 else None)
                        sp_of[kb] = S.v("SP")

                    def emit_exp_mask(kb, qi=qi, sp_of=sp_of, exp_done=exp_done,
                                      mask_done=mask_done, ptv=ptv):
                        pt = act_[:, 4096 + (kb % 2) * 1024:
                                  4096 + (kb % 2) * 1024 + 1024]
                        for kc in range(2):
                            par = 2 * (kb % 2) + kc
                            w = []
                            if kc == 0:
                                w = [("SP", sp_of[kb])] + guard(f"pt{kb % 2}")
                            S.op("s", lambda e, pt=pt, kc=kc, par=par: e.activation(
                                pt[:, kc * 512:(kc + 1) * 512], ps_[par][:, :],
                                AF.Exp, scale=ISQ),
                                waits=w, inc="SA" if kc == 1 else None)
                        sa_ = S.v("SA")
                        set_guard(f"ps{2 * (kb % 2)}", "SA")
                        set_guard(f"ps{2 * (kb % 2) + 1}", "SA")
                        exp_done[kb] = sa_
                        if kb >= 2 * qi:
                            j = kb - 2 * qi
                            pm = act_[:, 6144 + (kb % 2) * 1024:
                                      6144 + (kb % 2) * 1024 + 1024]
                            for kc in range(2):
                                w = [("SA", sa_)]
                                if kc == 0:
                                    w = w + guard(f"pm{kb % 2}")
                                S.op("v", lambda e, pm=pm, pt=pt, j=j, kc=kc:
                                     e.tensor_mul(pm[:, kc * 512:(kc + 1) * 512],
                                                  pt[:, kc * 512:(kc + 1) * 512],
                                                  dm_[:, (2 * j + kc) * 512:
                                                      (2 * j + kc + 1) * 512]),
                                     waits=w, inc="SV" if kc == 1 else None)
                            mask_done[kb] = S.v("SV")
                            ptv[kb] = pm
                        else:
                            mask_done[kb] = 0
                            ptv[kb] = pt

                    cb = [ps_[4], psB_][qi % 2]
                    bb = [psB_, ps_[4]][qi % 2]
                    cbk = ["ps4a", "psBa"][qi % 2]
                    bbk = ["psBa", "ps4a"][qi % 2]

                    def emit_ctx(kb, nkb=nkb, kv=kv, exp_done=exp_done,
                                 mask_done=mask_done, ptv=ptv, cb=cb, cbk=cbk):
                        first = kb == 0
                        last = kb == nkb - 1
                        pt = ptv[kb]
                        w = ([("SV", mask_done[kb])] if mask_done[kb]
                             else [("SA", exp_done[kb])])
                        if first:
                            w = w + guard(cbk) + guard("ps5a")
                        for kc in range(2):
                            S.op("t", lambda e, kb=kb, kc=kc, kv=kv, pt=pt, cb=cb:
                                 e.matmul(
                                     cb[:, :],
                                     kv[:, 4096 + (kb * 2 + kc) * 128:
                                        4096 + (kb * 2 + kc) * 128 + 128],
                                     pt[:, kc * 512:(kc + 1) * 512],
                                     start=(first and kc == 0),
                                     stop=(last and kc == 1)),
                                 waits=w if kc == 0 else [])
                            S.op("t", lambda e, kc=kc, pt=pt: e.matmul(
                                ps_[5][0:1, :], onesc_[:, 0:1],
                                pt[:, kc * 512:(kc + 1) * 512],
                                start=(first and kc == 0), stop=(last and kc == 1)),
                                inc="SP" if kc == 1 else None)
                        G[f"pt{kb % 2}"] = ("SP", S.v("SP"))
                        if mask_done[kb]:
                            G[f"pm{kb % 2}"] = ("SP", S.v("SP"))

                    emit_S(0)
                    for kb in range(nkb):
                        if kb + 1 < nkb:
                            emit_S(kb + 1)
                        emit_exp_mask(kb)
                        emit_ctx(kb)
                    sp_ctx = S.v("SP")
                    cf = ctxf_
                    S.op("v", lambda e, cf=cf, cb=cb: e.tensor_copy(
                        cf[:, :], cb[:, :]),
                         waits=[("SP", sp_ctx)] + guard("cf"), inc="SV")
                    set_guard(cbk, "SV")
                    S.op("v", lambda e: e.tensor_copy(rec_[0:1, :], ps_[5][0:1, :]),
                         waits=guard("recr"), inc="SV")
                    sv_den = S.v("SV")
                    set_guard("ps5a", "SV")
                    S.op("t", lambda e, bb=bb: e.matmul(bb[:, :], onesf_[:, :],
                                                 rec_[0:1, :], start=True, stop=True),
                         waits=[("SV", sv_den)] + guard(bbk), inc="SP")
                    sp_bc = S.v("SP")
                    S.op("v", lambda e, bb=bb: e.reciprocal(rec_[:, :], bb[:, :]),
                         waits=[("SP", sp_bc), ("SV", S.v("SV"))], inc="SV")
                    set_guard(bbk, "SV")
                    sg, sgk = next_stg()
                    S.op("v", lambda e, sg=sg, cf=cf: e.scalar_tensor_tensor(
                        sg[:, :], cf[:, :], 1.0, rec_[:, :], ALU.mult, ALU.mult),
                        waits=guard(sgk) + [("SV", S.v("SV"))], inc="SV")
                    sv_n = S.v("SV")
                    G["cf"] = ("SV", sv_n)
                    set_guard("recr", "SV")
                    rb = (qi // 2) * 512 + i * 128
                    y_store(lambda e, sg=sg, rb=rb, qi=qi: e.dma_start(
                        out=ctx_c[rb:rb + 128, (qi % 2) * 512:(qi % 2) * 512 + 512],
                        in_=sg[:, :]), waits=[("SV", sv_n)])
                    G[sgk] = ("SS", S.v("SS"))
                G[f"kv{i % 2}"] = ("SP", S.v("SP"))
                G[f"qsl{(i % 2) * 4096}"] = ("SP", S.v("SP"))
            set_guard("xn_r", "SP")
            set_guard("act_r", "SP")
            G["wo0"] = ("SP", S.v("SP"))
            G["wo8192"] = ("SP", S.v("SP"))
            load_wout(0)
            load_wout(1)
            ss_ctx = S.v("SS")
            S.op("g", lambda e: e.collective_compute(
                "AllGather", ALU.bypass, replica_groups=[list(range(NCORE))],
                ins=[ctx_c.ap()], outs=[ctx_x.ap()]),
                waits=[("SS", ss_ctx)], inc="SC")
            sc_ctx = S.v("SC")
            ctx_rdy = 0
            ctx_rdy_g = 0
            for gg in range(4):
                eng = "y" if gg < 2 else "g"
                w = [("SC", sc_ctx)]
                if gg in (0, 2):
                    w = w + guard("fm_r")

                def cload(e, gg=gg):
                    reg = S.ygreg if gg < 2 else S.greg
                    if gg in (0, 2):
                        e.reg_load(reg, meta_[0:1, 12:13])
                        if gg == 2:
                            e.reg_add(reg, reg, 2 * 2 * TC * TC)
                    else:
                        e.reg_add(reg, reg, 2 * TC * TC)
                    return e.dma_start(
                        out=fm_[:, gg * 4 * TC:(gg + 1) * 4 * TC].rearrange(
                            "p (m t) -> p m t", t=TC),
                        in_=bass.AP(ctx_x, reg,
                                    [[TC, 128], [128 * TC, 4], [1, TC]]))
                if gg < 2:
                    ctx_rdy = S.op("y", cload, waits=w, inc="SY", inc_n=16)
                else:
                    ctx_rdy_g = S.op("g", cload, waits=w, inc="SW", inc_n=16)
            for st in range(4):
                qs = (st % 2) * 8192
                for tc in range(NT):
                    par = cur["ps03"] % 4
                    cur["ps03"] += 1
                    pq = ps_[par]
                    for hc in range(HCH):
                        w = []
                        if hc == 0:
                            w = [("SW", wo_w[st]), ("SY", ctx_rdy),
                                 ("SW", ctx_rdy_g)] + guard(f"ps{par}")
                        S.op("t", lambda e, pq=pq, qs=qs, hc=hc, tc=tc: e.matmul(
                            pq[:, :],
                            fm_[:, hc * TC + tc * 128: hc * TC + tc * 128 + 128],
                            wp_[:, qs + hc * 512: qs + hc * 512 + 512],
                            start=(hc == 0), stop=False), waits=w)
                    S.op("t", lambda e, pq=pq, st=st: e.matmul(
                        pq[:, :], onesr_[0:1, 0:128],
                        bias4_[0:1, 1536 + st * 512: 1536 + (st + 1) * 512],
                        start=False, stop=True), inc="SP")
                    sp_o = S.v("SP")
                    if tc == NT - 1:
                        G[f"wo{qs}"] = ("SP", sp_o)
                        if st + 2 < 4:
                            load_wout(st + 2)
                    S.op("v", lambda e, tc=tc, st=st, pq=pq: e.tensor_add(
                        r_[:, tc * D + st * 512: tc * D + st * 512 + 512],
                        pq[:, :],
                        r_[:, tc * D + st * 512: tc * D + st * 512 + 512]),
                        waits=[("SP", sp_o)], inc="SV")
                    if st == 3:
                        rready[tc] = S.v("SV")
                    set_guard(f"ps{par}", "SV")
            set_guard("r", "SV")
            set_guard("fm_r", "SP")
            merge_guard("gu0", "gu0", "wo0")
            merge_guard("gu2048", "gu2048", "wo0")
            merge_guard("gu4096", "gu4096", "wo0")
            merge_guard("gu6144", "gu6144", "wo0")
            merge_guard("ds8192", "ds8192", "qs8192", "qs12288", "wo8192")
            merge_guard("ds13824", "ds13824", "qs12288", "wo8192")
            merge_guard("ps4", "ps4", "ps4a")
            merge_guard("ps5", "ps5", "ps5a")

        if "id" in stages:
            pass
        if "ln1" in stages:
            svf = emit_ln()
            for tc in range(NT):
                S.op("v", lambda e, tc=tc: e.tensor_copy(
                    r_[:, tc * D:(tc + 1) * D], xn_[:, tc * D:(tc + 1) * D]),
                    waits=[("SV", svf)], inc="SV")
            set_guard("r", "SV")
        if "hcb" in stages:
            emit_ffn(())
            S.op("v", lambda e: e.tensor_copy(r_[:, 0:512], scr_[:, :]), inc="SV")
            set_guard("r", "SV")
        if "ffn1" in stages:
            emit_ffn(experts)
        if "attn" in stages:
            emit_attn()
        if "ffn2" in stages:
            emit_ffn(experts)

        rg = G.get("r")
        y_store(lambda e: e.dma_start(
            out=y_t.ap().rearrange("(c p) f -> p c f", p=128),
            in_=r_[:, :].rearrange("p (c f) -> p c f", f=D)),
            waits=[rg] if rg else [])
        S.op("y", None, waits=[("SS", S.v("SS"))])

        with nc.Block() as block:
            @block.sync
            def _(e):
                with e.register("dynoffy") as reg:
                    S.ygreg = reg
                    S.run("y", e)

            @block.gpsimd
            def _(e):
                with e.register("dynoff") as reg:
                    S.greg = reg
                    S.run("g", e)

            @block.tensor
            def _(e):
                S.run("t", e)

            @block.vector
            def _(e):
                S.run("v", e)

            @block.scalar
            def _(e):
                S.run("s", e)

    return nc


def _host_prep(inputs):
    x = np.ascontiguousarray(np.asarray(inputs["L0_attn_x"], dtype=np.float32))
    wqkv = np.asarray(inputs["L0_attn_w_qkv"], np.float32)
    bqkv = np.asarray(inputs["L0_attn_b_qkv"], np.float32)
    wout = np.ascontiguousarray(np.asarray(inputs["L0_attn_w_out"], np.float32))
    bout = np.asarray(inputs["L0_attn_b_out"], np.float32)
    gate = np.asarray(inputs["L0_ffn_gate_weight"], np.float32)
    guw = np.asarray(inputs["L0_ffn_gate_up_weights"], np.float32)
    dww = np.asarray(inputs["L0_ffn_down_weights"], np.float32)
    sgu = np.asarray(inputs["L0_ffn_shared_gate_up"], np.float32)
    sdw = np.asarray(inputs["L0_ffn_shared_down"], np.float32)

    w3 = wqkv.reshape(D, NH, 3, 128)
    b3 = bqkv.reshape(NH, 3, 128)
    iden = np.eye(128, dtype=ml_dtypes.bfloat16)
    p = np.arange(128)[:, None]
    q = np.arange(512)[None, :]
    dmask = np.stack([(128 * i + p <= q) for i in range(4)], axis=1)
    dmask = np.ascontiguousarray(dmask.astype(ml_dtypes.bfloat16).reshape(128, 2048))

    xt = x.reshape(NCORE * TC, D)
    shared = dict(
        wo=wout,
        bo=np.ascontiguousarray(bout.reshape(1, D)),
        gateT=np.ascontiguousarray(gate.T),
        guT=np.ascontiguousarray(guw.transpose(0, 2, 1)),
        dT=np.ascontiguousarray(dww.transpose(0, 2, 1)),
        sguT=np.ascontiguousarray(sgu.T),
        sdT=np.ascontiguousarray(sdw.T),
        iden=iden, dmask=dmask)
    in_maps = []
    for c in range(NCORE):
        s = c // 4
        g4 = c % 4
        heads = [4 * g4 + i for i in range(HL)]
        wql = np.concatenate(
            [np.ascontiguousarray(w3[:, heads, k, :].reshape(D, 512))
             for k in range(3)], axis=1)
        bql = np.concatenate(
            [b3[heads, k, :].reshape(1, 512) for k in range(3)], axis=1)
        meta = np.zeros((1, 16), np.int32)
        meta[0, 12] = (4 * s * 2 * TC + g4 * 512) * TC
        meta[0, 13] = s * 4 * 2 * TC * TC
        m = dict(shared)
        m["x"] = np.ascontiguousarray(xt[TC * c: TC * (c + 1)])
        m["wqkvl"] = np.ascontiguousarray(wql)
        m["bqkvl"] = np.ascontiguousarray(bql)
        m["meta"] = meta
        in_maps.append(m)
    return in_maps


def kernel(**inputs):
    in_maps = _host_prep(inputs)
    nc = build_nc()
    res = run_bass_kernel_spmd(nc, in_maps, core_ids=list(range(NCORE)))
    out = np.concatenate([res.results[c]["y"] for c in range(NCORE)], axis=0)
    return out.reshape(2, SEQ, D)


if __name__ == "__main__":
    nc = build_nc()
    print("built ok; instructions:", nc.n_instructions())


# version 11 (final)
# speedup vs baseline: 1.0315x; 1.0045x over previous
import numpy as np
import ml_dtypes
from contextlib import ExitStack

import concourse.bass as bass
import concourse.mybir as mybir
from concourse.bass_utils import run_bass_kernel_spmd

F32 = mybir.dt.float32
BF16 = mybir.dt.bfloat16
I32 = mybir.dt.int32
AF = mybir.ActivationFunctionType
ALU = mybir.AluOpType
AX = mybir.AxisListType.X

NCORE = 8
D = 2048
TC = 1024
NT = 8
HCH = 16
FF = 1408
NFC = 11
NEXP = 10
NH = 16
HL = 4
SEQ = 4096
NQB = 8
ISQ = 0.08838834764831845

GU_SLOTS = [0, 2048, 4096, 6144]
D_SLOTS = [8192, 13824]
QS_SLOTS = [8192, 12288]
WP_ELS = 19456


class Sched:
    ENGINES = ("t", "v", "s", "g", "y")

    def __init__(self):
        self.q = {k: [] for k in self.ENGINES}
        self.cnt = {k: 0 for k in ("SW", "SY", "SS", "SP", "SV", "SA", "SG",
                                   "SC", "SC2")}
        self.sems = {}
        self.greg = None

    def op(self, eng, fn, waits=(), inc=None, inc_n=1):
        ws = tuple((s, v) for (s, v) in waits if s is not None and v and v > 0)
        if inc is not None:
            self.cnt[inc] += inc_n
        self.q[eng].append((ws, fn, inc, inc_n))
        return self.cnt[inc] if inc is not None else None

    def v(self, sem):
        return self.cnt[sem]

    def run(self, eng, e):
        waited = {}
        for ws, fn, inc, inc_n in self.q[eng]:
            for s, val in ws:
                if val > waited.get(s, 0):
                    e.wait_ge(self.sems[s], val)
                    waited[s] = val
            try:
                inst = fn(e) if fn is not None else None
            except Exception:
                import traceback
                traceback.print_exc()
                raise
            if inc is not None:
                inst.then_inc(self.sems[inc], inc_n)


def build_nc(stages=("ffn1", "attn", "ffn2"), experts=tuple(range(NEXP))):
    nc = bass.Bass("TRN2", target_bir_lowering=False)

    x_t = nc.dram_tensor("x", [TC, D], F32, kind="ExternalInput")
    wqkvl_t = nc.dram_tensor("wqkvl", [D, 1536], F32, kind="ExternalInput")
    bqkvl_t = nc.dram_tensor("bqkvl", [1, 1536], F32, kind="ExternalInput")
    wo_t = nc.dram_tensor("wo", [D, D], F32, kind="ExternalInput")
    bo_t = nc.dram_tensor("bo", [1, D], F32, kind="ExternalInput")
    gateT_t = nc.dram_tensor("gateT", [D, 8], F32, kind="ExternalInput")
    guT_t = nc.dram_tensor("guT", [8, D, 2 * FF], F32, kind="ExternalInput")
    dT_t = nc.dram_tensor("dT", [8, FF, D], F32, kind="ExternalInput")
    sguT_t = nc.dram_tensor("sguT", [D, 4 * FF], F32, kind="ExternalInput")
    sdT_t = nc.dram_tensor("sdT", [2 * FF, D], F32, kind="ExternalInput")
    iden_t = nc.dram_tensor("iden", [128, 128], BF16, kind="ExternalInput")
    dmask_t = nc.dram_tensor("dmask", [128, 2048], BF16, kind="ExternalInput")
    meta_t = nc.dram_tensor("meta", [1, 16], I32, kind="ExternalInput")
    y_t = nc.dram_tensor("y", [TC, D], F32, kind="ExternalOutput")

    xn2c = nc.dram_tensor("xn2c", [2 * TC, TC], BF16)
    xn2x = nc.dram_tensor("xn2x", [16 * TC, TC], BF16, addr_space="Shared")
    q_l = nc.dram_tensor("q_l", [512, SEQ], BF16)
    k_l = nc.dram_tensor("k_l", [512, SEQ], BF16)
    v_l = nc.dram_tensor("v_l", [SEQ, 512], BF16)
    ctx_c = nc.dram_tensor("ctx_c", [2 * TC, TC], BF16)
    ctx_x = nc.dram_tensor("ctx_x", [16 * TC, TC], BF16, addr_space="Shared")
    GROUPS = [[0, 1, 2, 3], [4, 5, 6, 7]]

    S = Sched()

    with ExitStack() as ectx:
        e_ = ectx.enter_context
        r_ = e_(nc.sbuf_tensor("r", [128, NT * D], F32))
        xn_ = e_(nc.sbuf_tensor("xn", [128, NT * D], BF16))
        fm_ = e_(nc.sbuf_tensor("fm", [128, HCH * TC], BF16))
        wp_ = e_(nc.sbuf_tensor("wp", [128, WP_ELS], BF16))
        act_ = e_(nc.sbuf_tensor("act", [128, NFC * TC], BF16))
        stg_ = [e_(nc.sbuf_tensor(f"stg{i}", [128, 512], BF16)) for i in range(2)]
        scr_ = e_(nc.sbuf_tensor("scr", [128, 512], F32))
        ctxf_ = e_(nc.sbuf_tensor("ctxf", [128, 512], F32))
        idn_ = e_(nc.sbuf_tensor("idn", [128, 128], BF16))
        dm_ = e_(nc.sbuf_tensor("dm", [128, 2048], BF16))
        gat_ = e_(nc.sbuf_tensor("gat", [128, HCH * 8], BF16))
        onesr_ = e_(nc.sbuf_tensor("onesr", [128, 512], BF16))
        onesc_ = e_(nc.sbuf_tensor("onesc", [128, 1], BF16))
        onesf_ = e_(nc.sbuf_tensor("onesf", [1, 128], F32))
        ceps_ = e_(nc.sbuf_tensor("ceps", [128, 1], F32))
        bias4_ = e_(nc.sbuf_tensor("bias4", [1, 3584], BF16))
        meta_ = e_(nc.sbuf_tensor("metas", [1, 16], I32))
        ps_ = [e_(nc.psum_tensor(f"ps{i}", [128, 512], F32)) for i in range(6)]
        psB_ = e_(nc.psum_tensor("psB", [128, 512], F32))
        psT_ = [e_(nc.psum_tensor("psT0", [128, 1024], BF16))]
        for name in S.cnt:
            S.sems[name] = e_(nc.semaphore(name))

        hcb_ = scr_[:, 0:64]
        rt_ = scr_[:, 64:112]
        lnst_ = scr_[:, 112:144]
        rec_ = scr_

        G = {}

        def guard(name):
            g = G.get(name)
            return [g] if g else []

        def set_guard(name, sem):
            G[name] = (sem, S.v(sem))

        def merge_guard(name, *names):
            best = None
            for n in names:
                g = G.get(n)
                if g and (best is None or g[1] > best[1]):
                    best = g
            if best:
                G[name] = best

        def g_dma(fn, waits=()):
            return S.op("g", fn, waits=waits, inc="SW", inc_n=16)

        def y_load(fn, waits=()):
            return S.op("y", fn, waits=waits, inc="SY", inc_n=16)

        def y_store(fn, waits=()):
            return S.op("y", fn, waits=waits, inc="SS", inc_n=16)

        for xq in range(4):
            y_load(lambda e, xq=xq: e.dma_start(
                out=r_[:, xq * 2 * D:(xq + 1) * 2 * D].rearrange(
                    "p (c f) -> p c f", f=D),
                in_=x_t.ap().rearrange("(c p) f -> p c f", p=128)[
                    :, xq * 2:(xq + 1) * 2, :]))
        y_load(lambda e: e.dma_start(out=idn_[:, :], in_=iden_t.ap()))
        y_load(lambda e: e.dma_start(out=dm_[:, :], in_=dmask_t.ap()))
        y_meta = y_load(lambda e: e.dma_start(out=meta_[:, :], in_=meta_t.ap()))
        S.op("g", lambda e: e.memset(onesr_[:, :], 1.0), inc="SG")
        S.op("g", lambda e: e.memset(onesc_[:, :], 1.0), inc="SG")
        S.op("g", lambda e: e.memset(onesf_[:, :], 1.0), inc="SG")
        S.op("g", lambda e: e.memset(ceps_[:, :], 1e-5), inc="SG")
        sg_pre = S.v("SG")
        g_dma(lambda e: e.dma_start(
            out=gat_[:, :].rearrange("p (c e) -> p c e", e=8),
            in_=gateT_t.ap().rearrange("(c p) e -> p c e", p=128)))
        g_dma(lambda e: e.dma_start(out=bias4_[0:1, 0:1536], in_=bqkvl_t.ap()))
        w_pre = g_dma(lambda e: e.dma_start(out=bias4_[0:1, 1536:3584], in_=bo_t.ap()))
        pre_t = [("SY", 112), ("SG", sg_pre), ("SW", w_pre)]
        rready = {}

        def rwait(tc):
            w = [("SY", 16 * (tc // 2 + 1)), ("SG", sg_pre), ("SW", w_pre)]
            if tc in rready:
                w.append(("SV", rready[tc]))
            else:
                w = w + guard("r")
            return w

        cur = {"gupair": 0, "dstrip": 0, "qs": 0, "ps03": 0, "stg": 0}

        def next_stg():
            i = cur["stg"] % 2
            cur["stg"] += 1
            return stg_[i], f"stg{i}"

        def emit_ln():
            def half_stats(half):
                base = half * 4
                for tc in range(base, base + 4):
                    rsl = r_[:, tc * D:(tc + 1) * D]
                    dump = xn_[:, tc * D:(tc + 1) * D]
                    w = rwait(tc)
                    if tc == 0:
                        w = w + guard("xn_r")
                    S.op("s", lambda e, o=dump, i=rsl, tc=tc: e.activation(
                        o, i, AF.Square,
                        accum_out=lnst_[:, 4 * tc + 1:4 * tc + 2]),
                        waits=w, inc="SA")
                    S.op("s", lambda e, o=dump, i=rsl, tc=tc: e.activation(
                        o, i, AF.Copy,
                        accum_out=lnst_[:, 4 * tc:4 * tc + 1]), inc="SA")
                sa_sums = S.v("SA")

                def st_(j):
                    return lnst_.rearrange("p (a b) -> p a b", b=4)[
                        :, base:base + 4, j:j + 1]

                def rt3(a, b):
                    return rt_[:, a:b].rearrange("p (a b) -> p a b", b=1)
                ra, rb = 4 * half, 4 * half + 4
                S.op("v", lambda e: e.tensor_scalar(st_(2), st_(0), -1.0 / D, None,
                                                    ALU.mult),
                     waits=[("SA", sa_sums), ("SV", S.v("SV"))], inc="SV")
                sv_negm = S.v("SV")
                S.op("s", lambda e: e.activation(rt3(ra, rb), st_(2), AF.Square),
                     waits=[("SV", sv_negm)], inc="SA")
                sa_msq = S.v("SA")
                S.op("v", lambda e: e.tensor_scalar(rt3(8 + ra, 8 + rb), st_(1),
                                                    1.0 / D, None, ALU.mult),
                     inc="SV")
                S.op("v", lambda e: e.tensor_sub(rt3(16 + ra, 16 + rb),
                                                 rt3(8 + ra, 8 + rb),
                                                 rt3(ra, rb)),
                     waits=[("SA", sa_msq), ("SV", S.v("SV"))], inc="SV")
                sv_var = S.v("SV")
                S.op("s", lambda e: e.activation(rt3(24 + ra, 24 + rb),
                                                 rt3(16 + ra, 16 + rb), AF.Ln,
                                                 bias=ceps_[:, :]),
                     waits=[("SV", sv_var)], inc="SA")
                S.op("s", lambda e: e.activation(st_(3), rt3(24 + ra, 24 + rb),
                                                 AF.Exp, scale=-0.5), inc="SA")
                sa_rstd = S.v("SA")
                for tc in range(base, base + 4):
                    w = [("SA", sa_rstd)]
                    if tc == 0:
                        w = w + guard("xn_r")
                    S.op("v", lambda e, tc=tc: e.tensor_scalar(
                        xn_[:, tc * D:(tc + 1) * D], r_[:, tc * D:(tc + 1) * D],
                        lnst_[:, 4 * tc + 2:4 * tc + 3],
                        lnst_[:, 4 * tc + 3:4 * tc + 4],
                        ALU.add, ALU.mult), waits=w, inc="SV")
                return S.v("SV")

            def half_trans(half, sv_xn):
                base = half * 4
                for hc in range(HCH):
                    pst = psT_[0]
                    for k in range(4):
                        tc = base + k
                        w = []
                        if k == 0:
                            w = [("SV", sv_xn), ("SY", 80)] + guard("psT0")
                            if hc == 0 and half == 0:
                                w = w + guard("fm_r")
                        S.op("t", lambda e, pst=pst, tc=tc, hc=hc, k=k:
                             e.transpose(
                                 pst[:, k * 128:(k + 1) * 128],
                                 xn_[:, tc * D + hc * 128: tc * D + hc * 128 + 128],
                                 idn_[:, :]),
                             waits=w, inc="SP" if k == 3 else None)
                    sp_tr = S.v("SP")
                    w = [("SP", sp_tr)]
                    if hc == 0 and half == 0:
                        w = w + guard("fm_r")
                    S.op("v", lambda e, hc=hc, half=half, pst=pst: e.tensor_copy(
                        fm_[:, hc * TC + half * 512: hc * TC + half * 512 + 512],
                        pst[:, 0:512]), waits=w, inc="SV")
                    set_guard("psT0", "SV")

            sv0 = half_stats(0)
            sv1 = half_stats(1)
            half_trans(0, sv0)
            half_trans(1, sv1)
            set_guard("xn_r", "SP")
            return S.v("SV")

        def emit_ffn(experts=tuple(range(NEXP))):
            sv_fm = emit_ln()
            last_sa_e = 0
            for tc in range(NT):
                psl = ps_[4 + tc % 2]
                for hc in range(HCH):
                    w = []
                    if hc == 0:
                        w = [("SV", sv_fm)] + guard(f"ps{4 + tc % 2}") + pre_t
                    S.op("t", lambda e, psl=psl, hc=hc, tc=tc: e.matmul(
                        psl[:, 0:8],
                        fm_[:, hc * TC + tc * 128: hc * TC + tc * 128 + 128],
                        gat_[:, hc * 8:(hc + 1) * 8],
                        start=(hc == 0), stop=(hc == HCH - 1)),
                        waits=w, inc="SP" if hc == HCH - 1 else None)
                sp_l = S.v("SP")

                def vs(fn, extra=()):
                    S.op("v", fn, waits=[("SV", S.v("SV"))] + list(extra), inc="SV")
                vs(lambda e, psl=psl: e.tensor_copy(rt_[:, 0:8], psl[:, 0:8]),
                   [("SP", sp_l), ("SA", last_sa_e)])
                set_guard(f"ps{4 + tc % 2}", "SV")
                vs(lambda e: e.reduce_max(rt_[:, 8:9], rt_[:, 0:8], axis=AX))
                vs(lambda e: e.tensor_scalar(rt_[:, 13:14], rt_[:, 8:9],
                                             -1.0, None, ALU.mult))
                sv_m1 = S.v("SV")
                S.op("s", lambda e: e.activation(rt_[:, 16:24], rt_[:, 0:8], AF.Exp,
                                                 bias=rt_[:, 13:14]),
                     waits=[("SV", sv_m1)], inc="SA")
                last_sa_e = S.v("SA")
                vs(lambda e: e.tensor_scalar(rt_[:, 24:32], rt_[:, 0:8],
                                             rt_[:, 8:9], None, ALU.is_ge))
                vs(lambda e: e.scalar_tensor_tensor(
                    rt_[:, 32:40], rt_[:, 24:32], -1e30, rt_[:, 0:8],
                    ALU.mult, ALU.add))
                vs(lambda e: e.reduce_max(rt_[:, 9:10], rt_[:, 32:40], axis=AX))
                vs(lambda e: e.tensor_scalar(rt_[:, 24:32], rt_[:, 0:8],
                                             rt_[:, 9:10], None, ALU.is_ge),
                   [("SA", last_sa_e)])
                vs(lambda e: e.reduce_sum(rt_[:, 10:11], rt_[:, 16:24], axis=AX))
                vs(lambda e: e.reciprocal(rt_[:, 11:12], rt_[:, 10:11]))
                vs(lambda e: e.tensor_scalar(rt_[:, 12:13], rt_[:, 11:12],
                                             0.5, None, ALU.mult))
                vs(lambda e, tc=tc: e.scalar_tensor_tensor(
                    hcb_[:, tc * 8:(tc + 1) * 8], rt_[:, 16:24], rt_[:, 12:13],
                    rt_[:, 24:32], ALU.mult, ALU.mult))
            loads = []
            for ex in experts:
                loads += [("gu", ex, fc) for fc in range(NFC)]
                loads += [("d", ex, hs) for hs in range(4)]
            linfo = {}

            def emit_load(item):
                kind, ex, j = item
                if kind == "gu":
                    k = cur["gupair"]
                    cur["gupair"] += 1
                    s1 = GU_SLOTS[(2 * k) % 4]
                    s2 = GU_SLOTS[(2 * k) % 4 + 1]
                    if ex < 8:
                        src1 = guT_t[ex, :, j * 128:(j + 1) * 128]
                        src2 = guT_t[ex, :, FF + j * 128: FF + j * 128 + 128]
                    else:
                        kk = ex - 8
                        src1 = sguT_t[:, kk * FF + j * 128: kk * FF + j * 128 + 128]
                        src2 = sguT_t[:, 2 * FF + kk * FF + j * 128:
                                      2 * FF + kk * FF + j * 128 + 128]
                    w1 = g_dma(lambda e, s=s1, src=src1: e.dma_start(
                        out=wp_[:, s:s + 2048].rearrange("p (c f) -> p c f", f=128),
                        in_=src.rearrange("(c p) f -> p c f", p=128)),
                        waits=guard(f"gu{s1}"))
                    w2 = g_dma(lambda e, s=s2, src=src2: e.dma_start(
                        out=wp_[:, s:s + 2048].rearrange("p (c f) -> p c f", f=128),
                        in_=src.rearrange("(c p) f -> p c f", p=128)),
                        waits=guard(f"gu{s2}"))
                    linfo[item] = (w1, w2, s1, s2)
                else:
                    ds = D_SLOTS[cur["dstrip"] % 2]
                    cur["dstrip"] += 1
                    if ex < 8:
                        src = dT_t[ex, :, j * 512:(j + 1) * 512]
                    else:
                        kk = ex - 8
                        src = sdT_t[kk * FF:(kk + 1) * FF, j * 512:(j + 1) * 512]
                    wd = g_dma(lambda e, ds=ds, src=src: e.dma_start(
                        out=wp_[:, ds:ds + 5632].rearrange("p (c h) -> p c h", h=512),
                        in_=src.rearrange("(c p) h -> p c h", p=128)),
                        waits=guard(f"ds{ds}"))
                    linfo[item] = (wd, ds)

            pumped = [0]

            def pump(n):
                while pumped[0] < min(n, len(loads)):
                    emit_load(loads[pumped[0]])
                    pumped[0] += 1

            pump(2)
            ui = 0
            for ex in experts:
                for fc in range(NFC):
                    w1, w2, s1, s2 = linfo[("gu", ex, fc)]
                    for th in range(2):
                        par = (fc * 2 + th) % 2
                        pa, pb = ps_[2 * par], ps_[2 * par + 1]
                        for hc in range(HCH):
                            w = []
                            if hc == 0:
                                w = [("SW", w1), ("SV", sv_fm)] + guard(f"ps{2 * par}")
                            S.op("t", lambda e, pa=pa, s1=s1, hc=hc, th=th: e.matmul(
                                pa[:, :], wp_[:, s1 + hc * 128: s1 + hc * 128 + 128],
                                fm_[:, hc * TC + th * 512: hc * TC + th * 512 + 512],
                                start=(hc == 0), stop=(hc == HCH - 1)), waits=w)
                        for hc in range(HCH):
                            w = []
                            if hc == 0:
                                w = [("SW", w2)] + guard(f"ps{2 * par + 1}")
                            S.op("t", lambda e, pb=pb, s2=s2, hc=hc, th=th: e.matmul(
                                pb[:, :], wp_[:, s2 + hc * 128: s2 + hc * 128 + 128],
                                fm_[:, hc * TC + th * 512: hc * TC + th * 512 + 512],
                                start=(hc == 0), stop=(hc == HCH - 1)),
                                waits=w, inc="SP" if hc == HCH - 1 else None)
                        sp_gu = S.v("SP")
                        if th == 1:
                            G[f"gu{s1}"] = ("SP", sp_gu)
                            G[f"gu{s2}"] = ("SP", sp_gu)
                        sg, sgk = next_stg()
                        S.op("s", lambda e, sg=sg, pa=pa: e.activation(
                            sg[:, :], pa[:, :], AF.Silu),
                            waits=[("SP", sp_gu)] + guard(sgk), inc="SA")
                        sa_si = S.v("SA")
                        set_guard(f"ps{2 * par}", "SA")
                        aw = [("SA", sa_si)]
                        if fc == 0 and th == 0:
                            aw = aw + guard("act_r")
                        S.op("v", lambda e, fc=fc, th=th, sg=sg, pb=pb: e.tensor_mul(
                            act_[:, fc * TC + th * 512: fc * TC + th * 512 + 512],
                            sg[:, :], pb[:, :]), waits=aw, inc="SV")
                        set_guard(f"ps{2 * par + 1}", "SV")
                        G[sgk] = ("SV", S.v("SV"))
                        if fc == NFC - 2 and th == 1:
                            sv_act_part = S.v("SV")
                    ui += 1
                    pump(ui + 2)
                sv_act = S.v("SV")
                for hs in range(4):
                    wd, ds = linfo[("d", ex, hs)]
                    for tc in range(NT):
                        par = 4 + (hs * NT + tc) % 2
                        pd = ps_[par]
                        for fc in range(NFC):
                            w = []
                            if fc == 0:
                                w = [("SW", wd), ("SV", sv_act_part)] + \
                                    guard(f"ps{par}")
                            elif fc == NFC - 1:
                                w = [("SV", sv_act)]
                            S.op("t", lambda e, pd=pd, fc=fc, tc=tc, ds=ds: e.matmul(
                                pd[:, :],
                                act_[:, fc * TC + tc * 128: fc * TC + tc * 128 + 128],
                                wp_[:, ds + fc * 512: ds + fc * 512 + 512],
                                start=(fc == 0), stop=(fc == NFC - 1)),
                                waits=w, inc="SP" if fc == NFC - 1 else None)
                        sp_d = S.v("SP")
                        if tc == NT - 1:
                            G[f"ds{ds}"] = ("SP", sp_d)
                        scal = (hcb_[:, tc * 8 + ex: tc * 8 + ex + 1]
                                if ex < 8 else 0.5)
                        S.op("v", lambda e, tc=tc, hs=hs, pd=pd, scal=scal:
                             e.scalar_tensor_tensor(
                                 r_[:, tc * D + hs * 512: tc * D + hs * 512 + 512],
                                 pd[:, :], scal,
                                 r_[:, tc * D + hs * 512: tc * D + hs * 512 + 512],
                                 ALU.mult, ALU.add),
                             waits=[("SP", sp_d)], inc="SV")
                        if experts and ex == experts[-1] and hs == 3:
                            rready[tc] = S.v("SV")
                        set_guard(f"ps{par}", "SV")
                    ui += 1
                    pump(ui + 2)
            set_guard("r", "SV")
            set_guard("fm_r", "SP")
            set_guard("act_r", "SP")

        def emit_attn():
            sv_fm = emit_ln()
            merge_guard("wpA", "gu0", "gu2048", "gu4096", "gu6144")
            merge_guard("wpB", "ds8192", "ds13824", "qs8192", "qs12288")
            sxc = y_store(lambda e: e.dma_start(
                out=xn2c.ap().rearrange("(c p) t -> p c t", p=128),
                in_=fm_[:, :].rearrange("p (c t) -> p c t", t=TC)),
                waits=[("SV", sv_fm)])
            S.op("g", lambda e: e.collective_compute(
                "AllGather", ALU.bypass, replica_groups=[list(range(NCORE))],
                ins=[xn2c.ap()], outs=[xn2x.ap()]),
                waits=[("SS", sxc)], inc="SC")
            sc_xn = S.v("SC")
            for xst in range(8):
                slot = (xst % 2) * 8192

                def xload(e, slot=slot, xst=xst):
                    e.reg_load(S.greg, meta_[0:1, 13:14])
                    e.reg_add(S.greg, S.greg,
                              (xst // 2) * 2 * TC * TC + (xst % 2) * 512)
                    return e.dma_start(
                        out=wp_[:, slot:slot + 8192].rearrange(
                            "p (c t) -> p c t", t=512),
                        in_=bass.AP(xn2x, S.greg,
                                    [[TC, 128], [128 * TC, 16], [1, 512]]))
                xw = g_dma(xload, waits=[("SC", sc_xn), ("SY", 112)] +
                           guard("wpA" if slot == 0 else "wpB"))
                for wst in range(6):
                    aslot = (wst % 2) * 4096
                    ww = g_dma(lambda e, aslot=aslot, wst=wst: e.dma_start(
                        out=act_[:, aslot:aslot + 4096].rearrange(
                            "p (c f) -> p c f", f=256),
                        in_=wqkvl_t[:, wst * 256:(wst + 1) * 256].rearrange(
                            "(c p) f -> p c f", p=128)),
                        waits=guard(f"aw{aslot}") +
                        (guard("act_r") if xst == 0 and wst < 2 else []))
                    if wst < 4:
                        for oc in range(2):
                            ocol = wst * 256 + oc * 128
                            par = cur["ps03"] % 4
                            cur["ps03"] += 1
                            pq = ps_[par]
                            for hc in range(HCH):
                                w = []
                                if hc == 0:
                                    w = [("SW", ww), ("SW", xw)] + guard(f"ps{par}")
                                S.op("t", lambda e, pq=pq, aslot=aslot, hc=hc,
                                     oc=oc, slot=slot: e.matmul(
                                         pq[:, :],
                                         act_[:, aslot + hc * 256 + oc * 128:
                                              aslot + hc * 256 + oc * 128 + 128],
                                         wp_[:, slot + hc * 512:
                                             slot + hc * 512 + 512],
                                         start=(hc == 0), stop=False), waits=w)
                            S.op("t", lambda e, pq=pq, ocol=ocol: e.matmul(
                                pq[:, :], bias4_[0:1, ocol:ocol + 128],
                                onesr_[0:1, 0:512], start=False, stop=True),
                                inc="SP")
                            sp_q = S.v("SP")
                            if oc == 1:
                                G[f"aw{aslot}"] = ("SP", sp_q)
                            sg, sgk = next_stg()
                            S.op("v", lambda e, sg=sg, pq=pq: e.tensor_copy(
                                sg[:, :], pq[:, :]),
                                waits=[("SP", sp_q)] + guard(sgk), inc="SV")
                            sv_c = S.v("SV")
                            set_guard(f"ps{par}", "SV")
                            dst = q_l if ocol < 512 else k_l
                            rb = ocol % 512
                            y_store(lambda e, sg=sg, dst=dst, rb=rb, xst=xst:
                                    e.dma_start(
                                        out=dst[rb:rb + 128,
                                                xst * 512:(xst + 1) * 512],
                                        in_=sg[:, :]), waits=[("SV", sv_c)])
                            G[sgk] = ("SS", S.v("SS"))
                    else:
                        vcol = (wst - 4) * 256
                        for tc4 in range(4):
                            par = cur["ps03"] % 4
                            cur["ps03"] += 1
                            pq = ps_[par]
                            for hc in range(HCH):
                                w = []
                                if hc == 0:
                                    w = [("SW", ww), ("SW", xw)] + guard(f"ps{par}")
                                S.op("t", lambda e, pq=pq, aslot=aslot, hc=hc,
                                     tc4=tc4, slot=slot: e.matmul(
                                         pq[:, 0:256],
                                         wp_[:, slot + hc * 512 + tc4 * 128:
                                             slot + hc * 512 + tc4 * 128 + 128],
                                         act_[:, aslot + hc * 256:
                                              aslot + hc * 256 + 256],
                                         start=(hc == 0), stop=False), waits=w)
                            S.op("t", lambda e, pq=pq, vcol=vcol: e.matmul(
                                pq[:, 0:256], onesr_[0:1, 0:128],
                                bias4_[0:1, 1024 + vcol: 1024 + vcol + 256],
                                start=False, stop=True), inc="SP")
                            sp_q = S.v("SP")
                            if tc4 == 3:
                                G[f"aw{aslot}"] = ("SP", sp_q)
                            sg, sgk = next_stg()
                            S.op("v", lambda e, sg=sg, pq=pq: e.tensor_copy(
                                sg[:, 0:256], pq[:, 0:256]),
                                waits=[("SP", sp_q)] + guard(sgk), inc="SV")
                            sv_c = S.v("SV")
                            set_guard(f"ps{par}", "SV")
                            y_store(lambda e, sg=sg, xst=xst, tc4=tc4, vcol=vcol:
                                    e.dma_start(
                                        out=v_l[xst * 512 + tc4 * 128:
                                                xst * 512 + tc4 * 128 + 128,
                                                vcol:vcol + 256],
                                        in_=sg[:, 0:256]), waits=[("SV", sv_c)])
                            G[sgk] = ("SS", S.v("SS"))
            sp_qkv = S.v("SP")
            ss_qkv = S.v("SS")
            G["qs8192"] = ("SP", sp_qkv)
            G["qs12288"] = ("SP", sp_qkv)
            G["qsl0"] = ("SP", sp_qkv)
            G["qsl4096"] = ("SP", sp_qkv)
            wo_w = [None] * 4

            def load_wout(st):
                qs = (st % 2) * 8192
                wo_w[st] = g_dma(lambda e, qs=qs, st=st: e.dma_start(
                    out=wp_[:, qs:qs + 8192].rearrange("p (c f) -> p c f", f=512),
                    in_=wo_t[:, st * 512:(st + 1) * 512].rearrange(
                        "(c p) f -> p c f", p=128)),
                    waits=guard(f"wo{qs}"))
            for i in range(HL):
                slot = (i % 2) * 8192
                kv = xn_[:, slot:slot + 8192]
                w0 = [("SS", ss_qkv)] + guard("xn_r") + guard(f"kv{i % 2}")
                g_dma(lambda e, i=i, kv=kv: e.dma_start(
                    out=kv[:, 0:4096], in_=k_l[i * 128:(i + 1) * 128, :]),
                    waits=w0)
                kv_rdy = g_dma(lambda e, i=i, kv=kv: e.dma_start(
                    out=kv[:, 4096:8192].rearrange("p (c d) -> p c d", d=128),
                    in_=bass.AP(v_l, i * 128,
                                [[512, 128], [512 * 128, 32], [1, 128]])),
                    waits=[])
                qsl = (i % 2) * 4096
                qcache = wp_[:, qsl:qsl + 4096]
                qw = g_dma(lambda e, i=i, qcache=qcache: e.dma_start(
                    out=qcache[:, :], in_=q_l[i * 128:(i + 1) * 128, :]),
                    waits=guard(f"qsl{qsl}"))
                for qi in range(NQB):
                    nkb = 2 * (qi + 1)
                    sp_of = {}
                    exp_done = {}
                    mask_done = {}
                    ptv = {}

                    def emit_S(kb, qi=qi, kv=kv, qw=qw, kv_rdy=kv_rdy,
                               sp_of=sp_of, qcache=qcache):
                        for kc in range(2):
                            par = 2 * (kb % 2) + kc
                            w = []
                            if kc == 0:
                                w = [("SW", kv_rdy), ("SW", qw)] + \
                                    guard(f"ps{2 * (kb % 2)}") + \
                                    guard(f"ps{2 * (kb % 2) + 1}")
                            S.op("t", lambda e, par=par, kb=kb, kc=kc, kv=kv,
                                 qi=qi, qcache=qcache: e.matmul(
                                     ps_[par][:, :],
                                     kv[:, kb * 256 + kc * 128:
                                        kb * 256 + kc * 128 + 128],
                                     qcache[:, qi * 512:(qi + 1) * 512],
                                     start=True, stop=True),
                                 waits=w, inc="SP" if kc == 1 else None)
                        sp_of[kb] = S.v("SP")

                    def emit_exp_mask(kb, qi=qi, sp_of=sp_of, exp_done=exp_done,
                                      mask_done=mask_done, ptv=ptv):
                        pt = act_[:, 4096 + (kb % 2) * 1024:
                                  4096 + (kb % 2) * 1024 + 1024]
                        for kc in range(2):
                            par = 2 * (kb % 2) + kc
                            w = []
                            if kc == 0:
                                w = [("SP", sp_of[kb])] + guard(f"pt{kb % 2}")
                            S.op("s", lambda e, pt=pt, kc=kc, par=par: e.activation(
                                pt[:, kc * 512:(kc + 1) * 512], ps_[par][:, :],
                                AF.Exp, scale=ISQ),
                                waits=w, inc="SA" if kc == 1 else None)
                        sa_ = S.v("SA")
                        set_guard(f"ps{2 * (kb % 2)}", "SA")
                        set_guard(f"ps{2 * (kb % 2) + 1}", "SA")
                        exp_done[kb] = sa_
                        if kb >= 2 * qi:
                            j = kb - 2 * qi
                            pm = act_[:, 6144 + (kb % 2) * 1024:
                                      6144 + (kb % 2) * 1024 + 1024]
                            for kc in range(2):
                                w = [("SA", sa_)]
                                if kc == 0:
                                    w = w + guard(f"pm{kb % 2}")
                                S.op("v", lambda e, pm=pm, pt=pt, j=j, kc=kc:
                                     e.tensor_mul(pm[:, kc * 512:(kc + 1) * 512],
                                                  pt[:, kc * 512:(kc + 1) * 512],
                                                  dm_[:, (2 * j + kc) * 512:
                                                      (2 * j + kc + 1) * 512]),
                                     waits=w, inc="SV" if kc == 1 else None)
                            mask_done[kb] = S.v("SV")
                            ptv[kb] = pm
                        else:
                            mask_done[kb] = 0
                            ptv[kb] = pt

                    cb = [ps_[4], psB_][qi % 2]
                    bb = [psB_, ps_[4]][qi % 2]
                    cbk = ["ps4a", "psBa"][qi % 2]
                    bbk = ["psBa", "ps4a"][qi % 2]

                    def emit_ctx(kb, nkb=nkb, kv=kv, exp_done=exp_done,
                                 mask_done=mask_done, ptv=ptv, cb=cb, cbk=cbk):
                        first = kb == 0
                        last = kb == nkb - 1
                        pt = ptv[kb]
                        w = ([("SV", mask_done[kb])] if mask_done[kb]
                             else [("SA", exp_done[kb])])
                        if first:
                            w = w + guard(cbk) + guard("ps5a")
                        for kc in range(2):
                            S.op("t", lambda e, kb=kb, kc=kc, kv=kv, pt=pt, cb=cb:
                                 e.matmul(
                                     cb[:, :],
                                     kv[:, 4096 + (kb * 2 + kc) * 128:
                                        4096 + (kb * 2 + kc) * 128 + 128],
                                     pt[:, kc * 512:(kc + 1) * 512],
                                     start=(first and kc == 0),
                                     stop=(last and kc == 1)),
                                 waits=w if kc == 0 else [])
                            S.op("t", lambda e, kc=kc, pt=pt: e.matmul(
                                ps_[5][0:1, :], onesc_[:, 0:1],
                                pt[:, kc * 512:(kc + 1) * 512],
                                start=(first and kc == 0), stop=(last and kc == 1)),
                                inc="SP" if kc == 1 else None)
                        G[f"pt{kb % 2}"] = ("SP", S.v("SP"))
                        if mask_done[kb]:
                            G[f"pm{kb % 2}"] = ("SP", S.v("SP"))

                    emit_S(0)
                    for kb in range(nkb):
                        if kb + 1 < nkb:
                            emit_S(kb + 1)
                        emit_exp_mask(kb)
                        emit_ctx(kb)
                    sp_ctx = S.v("SP")
                    cf = ctxf_
                    S.op("s", lambda e, cf=cf, cb=cb: e.activation(
                        cf[:, :], cb[:, :], AF.Copy),
                         waits=[("SP", sp_ctx)] + guard("cf"), inc="SA")
                    sa_cf = S.v("SA")
                    set_guard(cbk, "SA")
                    S.op("v", lambda e: e.tensor_copy(rec_[0:1, :], ps_[5][0:1, :]),
                         waits=[("SP", sp_ctx)] + guard("recr"), inc="SV")
                    sv_den = S.v("SV")
                    set_guard("ps5a", "SV")
                    S.op("t", lambda e, bb=bb: e.matmul(bb[:, :], onesf_[:, :],
                                                 rec_[0:1, :], start=True, stop=True),
                         waits=[("SV", sv_den)] + guard(bbk), inc="SP")
                    sp_bc = S.v("SP")
                    S.op("v", lambda e, bb=bb: e.reciprocal(rec_[:, :], bb[:, :]),
                         waits=[("SP", sp_bc), ("SV", S.v("SV"))], inc="SV")
                    set_guard(bbk, "SV")
                    sg, sgk = next_stg()
                    S.op("v", lambda e, sg=sg, cf=cf: e.tensor_mul(
                        sg[:, :], cf[:, :], rec_[:, :]),
                        waits=[("SA", sa_cf)] + guard(sgk) +
                        [("SV", S.v("SV"))], inc="SV")
                    sv_n = S.v("SV")
                    G["cf"] = ("SV", sv_n)
                    set_guard("recr", "SV")
                    rb = (qi // 2) * 512 + i * 128
                    y_store(lambda e, sg=sg, rb=rb, qi=qi: e.dma_start(
                        out=ctx_c[rb:rb + 128, (qi % 2) * 512:(qi % 2) * 512 + 512],
                        in_=sg[:, :]), waits=[("SV", sv_n)])
                    G[sgk] = ("SS", S.v("SS"))
                G[f"kv{i % 2}"] = ("SP", S.v("SP"))
                G[f"qsl{(i % 2) * 4096}"] = ("SP", S.v("SP"))
            set_guard("xn_r", "SP")
            set_guard("act_r", "SP")
            G["wo0"] = ("SP", S.v("SP"))
            G["wo8192"] = ("SP", S.v("SP"))
            load_wout(0)
            load_wout(1)
            ss_ctx = S.v("SS")
            S.op("g", lambda e: e.collective_compute(
                "AllGather", ALU.bypass, replica_groups=[list(range(NCORE))],
                ins=[ctx_c.ap()], outs=[ctx_x.ap()]),
                waits=[("SS", ss_ctx)], inc="SC")
            sc_ctx = S.v("SC")
            ctx_rdy = 0
            ctx_rdy_g = 0
            for gg in range(4):
                eng = "y" if gg < 2 else "g"
                w = [("SC", sc_ctx)]
                if gg in (0, 2):
                    w = w + guard("fm_r")

                def cload(e, gg=gg):
                    reg = S.ygreg if gg < 2 else S.greg
                    if gg in (0, 2):
                        e.reg_load(reg, meta_[0:1, 12:13])
                        if gg == 2:
                            e.reg_add(reg, reg, 2 * 2 * TC * TC)
                    else:
                        e.reg_add(reg, reg, 2 * TC * TC)
                    return e.dma_start(
                        out=fm_[:, gg * 4 * TC:(gg + 1) * 4 * TC].rearrange(
                            "p (m t) -> p m t", t=TC),
                        in_=bass.AP(ctx_x, reg,
                                    [[TC, 128], [128 * TC, 4], [1, TC]]))
                if gg < 2:
                    ctx_rdy = S.op("y", cload, waits=w, inc="SY", inc_n=16)
                else:
                    ctx_rdy_g = S.op("g", cload, waits=w, inc="SW", inc_n=16)
            for st in range(4):
                qs = (st % 2) * 8192
                for tc in range(NT):
                    par = cur["ps03"] % 4
                    cur["ps03"] += 1
                    pq = ps_[par]
                    for hc in range(HCH):
                        w = []
                        if hc == 0:
                            w = [("SW", wo_w[st]), ("SY", ctx_rdy),
                                 ("SW", ctx_rdy_g)] + guard(f"ps{par}")
                        S.op("t", lambda e, pq=pq, qs=qs, hc=hc, tc=tc: e.matmul(
                            pq[:, :],
                            fm_[:, hc * TC + tc * 128: hc * TC + tc * 128 + 128],
                            wp_[:, qs + hc * 512: qs + hc * 512 + 512],
                            start=(hc == 0), stop=False), waits=w)
                    S.op("t", lambda e, pq=pq, st=st: e.matmul(
                        pq[:, :], onesr_[0:1, 0:128],
                        bias4_[0:1, 1536 + st * 512: 1536 + (st + 1) * 512],
                        start=False, stop=True), inc="SP")
                    sp_o = S.v("SP")
                    if tc == NT - 1:
                        G[f"wo{qs}"] = ("SP", sp_o)
                        if st + 2 < 4:
                            load_wout(st + 2)
                    S.op("v", lambda e, tc=tc, st=st, pq=pq: e.tensor_add(
                        r_[:, tc * D + st * 512: tc * D + st * 512 + 512],
                        pq[:, :],
                        r_[:, tc * D + st * 512: tc * D + st * 512 + 512]),
                        waits=[("SP", sp_o)], inc="SV")
                    if st == 3:
                        rready[tc] = S.v("SV")
                    set_guard(f"ps{par}", "SV")
            set_guard("r", "SV")
            set_guard("fm_r", "SP")
            merge_guard("gu0", "gu0", "wo0")
            merge_guard("gu2048", "gu2048", "wo0")
            merge_guard("gu4096", "gu4096", "wo0")
            merge_guard("gu6144", "gu6144", "wo0")
            merge_guard("ds8192", "ds8192", "qs8192", "qs12288", "wo8192")
            merge_guard("ds13824", "ds13824", "qs12288", "wo8192")
            merge_guard("ps4", "ps4", "ps4a")
            merge_guard("ps5", "ps5", "ps5a")

        if "id" in stages:
            pass
        if "ln1" in stages:
            svf = emit_ln()
            for tc in range(NT):
                S.op("v", lambda e, tc=tc: e.tensor_copy(
                    r_[:, tc * D:(tc + 1) * D], xn_[:, tc * D:(tc + 1) * D]),
                    waits=[("SV", svf)], inc="SV")
            set_guard("r", "SV")
        if "hcb" in stages:
            emit_ffn(())
            S.op("v", lambda e: e.tensor_copy(r_[:, 0:512], scr_[:, :]), inc="SV")
            set_guard("r", "SV")
        if "ffn1" in stages:
            emit_ffn(experts)
        if "attn" in stages:
            emit_attn()
        if "ffn2" in stages:
            emit_ffn(experts)

        rg = G.get("r")
        y_store(lambda e: e.dma_start(
            out=y_t.ap().rearrange("(c p) f -> p c f", p=128),
            in_=r_[:, :].rearrange("p (c f) -> p c f", f=D)),
            waits=[rg] if rg else [])
        S.op("y", None, waits=[("SS", S.v("SS"))])

        with nc.Block() as block:
            @block.sync
            def _(e):
                with e.register("dynoffy") as reg:
                    S.ygreg = reg
                    S.run("y", e)

            @block.gpsimd
            def _(e):
                with e.register("dynoff") as reg:
                    S.greg = reg
                    S.run("g", e)

            @block.tensor
            def _(e):
                S.run("t", e)

            @block.vector
            def _(e):
                S.run("v", e)

            @block.scalar
            def _(e):
                S.run("s", e)

    return nc


def _host_prep(inputs):
    x = np.ascontiguousarray(np.asarray(inputs["L0_attn_x"], dtype=np.float32))
    wqkv = np.asarray(inputs["L0_attn_w_qkv"], np.float32)
    bqkv = np.asarray(inputs["L0_attn_b_qkv"], np.float32)
    wout = np.ascontiguousarray(np.asarray(inputs["L0_attn_w_out"], np.float32))
    bout = np.asarray(inputs["L0_attn_b_out"], np.float32)
    gate = np.asarray(inputs["L0_ffn_gate_weight"], np.float32)
    guw = np.asarray(inputs["L0_ffn_gate_up_weights"], np.float32)
    dww = np.asarray(inputs["L0_ffn_down_weights"], np.float32)
    sgu = np.asarray(inputs["L0_ffn_shared_gate_up"], np.float32)
    sdw = np.asarray(inputs["L0_ffn_shared_down"], np.float32)

    w3 = wqkv.reshape(D, NH, 3, 128)
    b3 = bqkv.reshape(NH, 3, 128)
    iden = np.eye(128, dtype=ml_dtypes.bfloat16)
    p = np.arange(128)[:, None]
    q = np.arange(512)[None, :]
    dmask = np.stack([(128 * i + p <= q) for i in range(4)], axis=1)
    dmask = np.ascontiguousarray(dmask.astype(ml_dtypes.bfloat16).reshape(128, 2048))

    xt = x.reshape(NCORE * TC, D)
    shared = dict(
        wo=wout,
        bo=np.ascontiguousarray(bout.reshape(1, D)),
        gateT=np.ascontiguousarray(gate.T),
        guT=np.ascontiguousarray(guw.transpose(0, 2, 1)),
        dT=np.ascontiguousarray(dww.transpose(0, 2, 1)),
        sguT=np.ascontiguousarray(sgu.T),
        sdT=np.ascontiguousarray(sdw.T),
        iden=iden, dmask=dmask)
    in_maps = []
    for c in range(NCORE):
        s = c // 4
        g4 = c % 4
        heads = [4 * g4 + i for i in range(HL)]
        wql = np.concatenate(
            [np.ascontiguousarray(w3[:, heads, k, :].reshape(D, 512))
             for k in range(3)], axis=1)
        bql = np.concatenate(
            [b3[heads, k, :].reshape(1, 512) for k in range(3)], axis=1)
        meta = np.zeros((1, 16), np.int32)
        meta[0, 12] = (4 * s * 2 * TC + g4 * 512) * TC
        meta[0, 13] = s * 4 * 2 * TC * TC
        m = dict(shared)
        m["x"] = np.ascontiguousarray(xt[TC * c: TC * (c + 1)])
        m["wqkvl"] = np.ascontiguousarray(wql)
        m["bqkvl"] = np.ascontiguousarray(bql)
        m["meta"] = meta
        in_maps.append(m)
    return in_maps


def kernel(**inputs):
    in_maps = _host_prep(inputs)
    nc = build_nc()
    res = run_bass_kernel_spmd(nc, in_maps, core_ids=list(range(NCORE)))
    out = np.concatenate([res.results[c]["y"] for c in range(NCORE)], axis=0)
    return out.reshape(2, SEQ, D)


if __name__ == "__main__":
    nc = build_nc()
    print("built ok; instructions:", nc.n_instructions())
